# version 16
# baseline (speedup 1.0000x reference)
import numpy as np
from contextlib import ExitStack

import concourse.bass as bass
import concourse.mybir as mybir
from concourse.bass import IndirectOffsetOnAxis
from concourse.bass_utils import run_bass_kernel_spmd

F32 = mybir.dt.float32
BF16 = mybir.dt.bfloat16
I32 = mybir.dt.int32
OP = mybir.AluOpType
ACT = mybir.ActivationFunctionType
AX = mybir.AxisListType

NCORES = 8
B, L, DM, DI, DS, RK = 8, 4096, 1024, 2048, 16, 64
DMT, DIT = DM // 128, DI // 128
TC, NCH = 256, L // 256
TF = 512
ET = 2 * DI // 128
EPS = 1e-5

ENGINES = ("sync", "vector", "scalar", "tensor", "gpsimd")


class Sem:
    def __init__(self, h):
        self.h = h
        self.n = 0

    def inc(self, k=1):
        self.n += k
        return self.n


class Bank:

    def __init__(self, ps, ready, free_s, free_v):
        self.ps = ps
        self.ready = ready
        self.free_s = free_s
        self.free_v = free_v


class Arena:

    def __init__(self, ap):
        self.ap = ap
        self.cap = ap.shape[1]
        self.off = 0
        self.marks = []

    def _take(self, words, align=16):
        o = (self.off + align - 1) // align * align
        assert o + words <= self.cap, f"arena overflow {o}+{words}>{self.cap}"
        self.off = o + words
        return o

    def f32(self, p, words):
        o = self._take(words)
        return self.ap[0:p, o:o + words]

    def bf16(self, p, n):
        words = (n + 1) // 2
        o = self._take(words)
        return self.ap[0:p, o:o + words].bitcast(BF16)[:, 0:n]

    def i32(self, p, n):
        o = self._take(n)
        return self.ap[0:p, o:o + n].bitcast(I32)

    def mark(self):
        self.marks.append(self.off)

    def reset(self):
        self.off = self.marks.pop()


class C:
    def __init__(self, nc):
        self.nc = nc
        self.ops = {e: [] for e in ENGINES}

    def op(self, eng, fn, inc=None, k=1, dma=False):
        if inc is not None:
            inc.inc(k)
            self.ops[eng].append(
                lambda E, fn=fn, h=inc.h, k=k: fn(E).then_inc(h, k))
            if eng in ("vector", "scalar", "tensor") and not dma:
                self.ops[eng].append(
                    lambda E, h=inc.h, vv=inc.n: E.wait_ge(h, vv))
        else:
            self.ops[eng].append(fn)

    def sync_dma(self, fn):
        self.op("sync", fn, inc=self.dL, k=16)
        self.wait("sync", self.dL)

    def gp_dma(self, fn):
        self.op("gpsimd", fn, inc=self.dG, k=16)
        self.wait("gpsimd", self.dG)

    def pipe_dma(self, eng, fn):
        sems = self.pd[eng]
        sem = sems[self.pd_i[eng] % 4]
        self.pd_i[eng] += 1
        self.wait(eng, sem)
        self.op(eng, fn, inc=sem, k=16)
        cnt = sem.n

        def w(eng2, sem=sem, cnt=cnt):
            self.wait(eng2, sem, cnt)
        return w

    def pipe_drain(self, eng2):
        for eng in ("sync", "gpsimd"):
            for sem in self.pd[eng]:
                self.wait(eng2, sem)

    def wait_free(self, eng, bank):
        self.wait(eng, bank.free_s)
        self.wait(eng, bank.free_v)

    def wait(self, eng, sem, v=None):
        vv = sem.n if v is None else v
        if vv > 0:
            self.ops[eng].append(lambda E, h=sem.h, vv=vv: E.wait_ge(h, vv))



def emit_setup(c):
    for fn in (
        lambda E: E.memset(c.ones_col, 1.0),
        lambda E: E.memset(c.ones_col_bf, 1.0),
        lambda E: E.memset(c.ones_row, 1.0),
        lambda E: E.memset(c.carry, 0.0),
        lambda E: E.memset(c.uf, 0.0),
    ):
        c.op("vector", fn)

    small = [
        (c.A_sb, c.t_alog), (c.gam, c.t_gamma), (c.bet, c.t_beta),
        (c.convb, c.t_convb), (c.bdt, c.t_bdt), (c.Dsk, c.t_dsk),
        (c.bf_c, c.t_bf), (c.bi_c, c.t_bi), (c.bc_c, c.t_bc), (c.bo_c, c.t_bo),
        (c.sel16B, c.t_sel16), (c.sel16C, c.t_sel16), (c.gidx, c.t_gidx),
    ]
    for dst, src in small:
        c.sync_dma(lambda E, d=dst, s=src: E.dma_start(d, s.ap()))
    small_loaded = c.dL.n
    c.wait("scalar", c.dL, small_loaded)
    c.op("scalar", lambda E: E.activation(c.A_sb, c.A_sb, ACT.Exp), inc=c.sS)
    c.wait("vector", c.sS)
    c.op("vector", lambda E: E.tensor_scalar(c.A_sb, c.A_sb, -1.0, None, OP.mult))
    c.wait("vector", c.dL, small_loaded)
    c.op("vector", lambda E: E.tensor_scalar(c.ngam, c.gam, -1.0, None, OP.mult),
         inc=c.sV)

    sc_f, sc_b = c.setup_f, c.setup_b

    def cast_stream(src_dram, n_rows, n_cols, dst_dram=None, dst_sb=None):
        for r2 in range(n_rows // 128):
            c.wait("sync", c.sV)
            c.sync_dma(lambda E, s=src_dram, r2=r2, w=n_cols: E.dma_start(
                sc_f[:, 0:w], bass.AP(s, r2 * 128 * w, [[w, 128], [1, w]])))
            c.wait("vector", c.dL)
            c.op("vector", lambda E, w=n_cols: E.tensor_copy(
                sc_b[:, 0:w], sc_f[:, 0:w]), inc=c.sV)
            c.wait("sync", c.sV)
            if dst_dram is not None:
                c.sync_dma(lambda E, d=dst_dram, r2=r2, w=n_cols: E.dma_start(
                    bass.AP(d, r2 * 128 * w, [[w, 128], [1, w]]), sc_b[:, 0:w]))
            else:
                c.sync_dma(lambda E, d=dst_sb, r2=r2, w=n_cols: E.dma_start(
                    d[:, r2 * w:(r2 + 1) * w], sc_b[:, 0:w]))

    cast_stream(c.t_win, DM, 4096, dst_dram=c.d_winbf)
    cast_stream(c.t_wout, DI, 1024, dst_dram=c.d_woutbf)
    cast_stream(c.t_wxp, DI, 128, dst_sb=c.wxp_sb)
    c.wait("sync", c.sV)
    c.sync_dma(lambda E: E.dma_start(sc_f[64:128, 0:DI], c.t_wdt.ap()))
    c.wait("vector", c.dL)
    c.op("vector", lambda E: E.tensor_copy(c.wdt_sb, sc_f[64:128, 0:DI]), inc=c.sV)
    c.wait("sync", c.sV)
    c.sync_dma(lambda E: E.dma_start(sc_f[:, 0:128], c.t_ident.ap()))
    c.wait("vector", c.dL)
    c.op("vector", lambda E: E.tensor_copy(c.ident_bf, sc_f[:, 0:128]), inc=c.sV)
    c.wait("sync", c.sV)
    for t_src, d_dst in c.ffn_casts:
        cast_stream(t_src, DM, 1024, dst_dram=d_dst)



def _emit_prod_ln(c, ch, par):
    if c.strong_barrier and ch == 0:
        for eng in ("sync", "vector", "scalar", "tensor", "gpsimd"):
            c.wait(eng, c.sV, c.sV.n)
            c.wait(eng, c.sS, c.sS.n)
            c.wait(eng, c.sT, c.sT.n)
            c.wait(eng, c.dL, c.dL.n)

    def vop(fn, inc=None):
        c.op("vector", fn, inc=inc)

    c.wait("sync", c.sV, c.x_free_v)
    c.op("sync", lambda E, ch=ch: E.dma_start(
        c.x_sb.rearrange("p (k t) -> p k t", k=DMT),
        bass.AP(c.t_x, ch * TC, [[L, 128], [128 * L, DMT], [1, TC]])),
        inc=c.dX, k=16)
    x_loaded = c.dX.n

    c.wait("scalar", c.dX, x_loaded)
    c.op("scalar", lambda E: E.activation(c.xsq_bf, c.x_sb, ACT.Square), inc=c.sS)
    xsq_done = c.sS.n
    bS, bS2 = c.banks[0], c.banks[1]
    c.wait_free("tensor", bS)
    c.wait("tensor", c.dX, x_loaded)
    for kt in range(DMT):
        c.op("tensor", lambda E, kt=kt: E.matmul(
            bS.ps[0:1, 0:TC], c.ones_col, c.x_sb[:, kt * TC:(kt + 1) * TC],
            start=(kt == 0), stop=(kt == DMT - 1)),
            inc=(bS.ready if kt == DMT - 1 else None))
    c.wait_free("tensor", bS2)
    c.wait("tensor", c.sS, xsq_done)
    for kt in range(DMT):
        c.op("tensor", lambda E, kt=kt: E.matmul(
            bS2.ps[0:1, 0:TC], c.ones_col_bf, c.xsq_bf[:, kt * TC:(kt + 1) * TC],
            start=(kt == 0), stop=(kt == DMT - 1)),
            inc=(bS2.ready if kt == DMT - 1 else None))

    r = c.rows
    c.wait("vector", bS.ready)
    c.wait("vector", bS2.ready)
    vop(lambda E: E.tensor_scalar(r["mu"], bS.ps[0:1, 0:TC], 1.0 / DM, None,
                                  OP.mult), inc=bS.free_v)
    vop(lambda E: E.tensor_scalar(r["ex2"], bS2.ps[0:1, 0:TC], 1.0 / DM, None,
                                  OP.mult), inc=bS2.free_v)
    vop(lambda E: E.tensor_tensor(r["mu2"], r["mu"], r["mu"], OP.mult))
    vop(lambda E: E.tensor_tensor(r["var"], r["ex2"], r["mu2"], OP.subtract))
    vop(lambda E: E.tensor_scalar(r["veps"], r["var"], EPS, None, OP.add))
    vop(lambda E: E.reciprocal(r["inv"], r["veps"]))
    vop(lambda E: E.tensor_scalar(r["vfrac"], r["inv"], -EPS, 1.0, OP.mult, OP.add))
    vop(lambda E: E.tensor_scalar(r["veps2"], r["vfrac"], EPS, None, OP.add))
    vop(lambda E: E.reciprocal(r["inv2"], r["veps2"]), inc=c.sV)
    inv_done = c.sV.n
    c.wait("scalar", c.sV, inv_done)
    c.op("scalar", lambda E: E.activation(r["r1"], r["inv"], ACT.Sqrt))
    c.op("scalar", lambda E: E.activation(r["r2"], r["inv2"], ACT.Sqrt), inc=c.sS)
    s_r2 = c.sS.n
    c.wait("vector", c.sS, s_r2)
    vop(lambda E: E.tensor_tensor(r["R"], r["r1"], r["r2"], OP.mult))
    vop(lambda E: E.tensor_tensor(r["muR"], r["mu"], r["R"], OP.mult))
    vop(lambda E: E.tensor_scalar(r["onepr"], r["r1"], 1.0, None, OP.add))
    vop(lambda E: E.tensor_tensor(r["mur"], r["mu"], r["r1"], OP.mult), inc=c.sV)
    rows_done = c.sV.n

    bA, bB = c.banks[2], c.banks[3]

    def bcast(bank, row):
        c.wait_free("tensor", bank)
        c.wait("tensor", c.sV, rows_done)
        c.op("tensor", lambda E, bank=bank, row=row: E.matmul(
            bank.ps[:, 0:TC], c.ones_row, row, start=True, stop=True),
            inc=bank.ready)

    xpn = c.xpn2[par]
    bcast(bA, r["onepr"])
    bcast(bB, r["mur"])
    c.wait("vector", bA.ready)
    c.wait("vector", bB.ready)
    c.wait("vector", c.dL, c.xpn_free_dl[par])
    for kt in range(DMT):
        sl = slice(kt * TC, (kt + 1) * TC)
        vop(lambda E, sl=sl, xpn=xpn: E.scalar_tensor_tensor(
            xpn[:, sl], c.x_sb[:, sl], 1.0, bA.ps[:, 0:TC], OP.mult, OP.mult))
        vop(lambda E, sl=sl, xpn=xpn: E.tensor_tensor(
            xpn[:, sl], xpn[:, sl], bB.ps[:, 0:TC], OP.subtract),
            inc=(bA.free_v if kt == DMT - 1 else None))
    vop(lambda E: E.nop(), inc=bB.free_v)

    bcast(bA, r["R"])
    bcast(bB, r["muR"])
    c.wait("vector", bA.ready)
    c.wait("vector", bB.ready)
    for kt in range(DMT):
        sl = slice(kt * TC, (kt + 1) * TC)
        vop(lambda E, sl=sl, kt=kt: E.scalar_tensor_tensor(
            c.tmp1, c.x_sb[:, sl], c.gam[:, kt:kt + 1], bA.ps[:, 0:TC],
            OP.mult, OP.mult))
        vop(lambda E, sl=sl, kt=kt: E.scalar_tensor_tensor(
            c.tmp2, bB.ps[:, 0:TC], c.ngam[:, kt:kt + 1], c.tmp1,
            OP.mult, OP.add))
        vop(lambda E, sl=sl, kt=kt: E.tensor_scalar(
            c.xn2bf[:, sl], c.tmp2, c.bet[:, kt:kt + 1], None, OP.add),
            inc=(bA.free_v if kt == DMT - 1 else None))
    vop(lambda E: E.nop(), inc=bB.free_v)
    vop(lambda E: E.nop(), inc=c.sV)
    c.xn2_done_v = c.sV.n
    c.x_free_v = c.xn2_done_v


def _prod_u_xz(c, ch, par, j):
    upf = c.upf2[par]
    siluz = c.siluz2[par]
    wpar = j % 2
    wb = c.win_buf[wpar]
    dw = c.dW0 if wpar == 0 else c.dW1
    c.wait("sync", c.sT, c.win_free_t[wpar])
    c.op("sync", lambda E, j=j, wb=wb: E.dma_start(
        wb.rearrange("p (k q) -> p k q", k=DMT),
        bass.AP(c.d_winbf, j * 128, [[4096, 128], [128 * 4096, DMT], [1, 128]])),
        inc=dw, k=16)
    wslab = dw.n
    bank = c.banks[4 + (j % 2)]
    c.wait_free("tensor", bank)
    c.wait("tensor", dw, wslab)
    c.wait("tensor", c.sV, c.xn2_done_v)
    for kt in range(DMT):
        c.op("tensor", lambda E, kt=kt, wb=wb, bank=bank: E.matmul(
            bank.ps[:, 0:TC], wb[:, kt * 128:(kt + 1) * 128],
            c.xn2bf[:, kt * TC:(kt + 1) * TC],
            start=(kt == 0), stop=(kt == DMT - 1)),
            inc=(bank.ready if kt == DMT - 1 else None))
    c.op("tensor", lambda E: E.nop(), inc=c.sT)
    c.win_free_t[wpar] = c.sT.n
    c.wait("scalar", bank.ready)
    if j < DIT:
        base = j * (TC + 4)
        c.op("scalar", lambda E, base=base, bank=bank: E.copy(
            c.uf[:, base + 4: base + 4 + TC], bank.ps[:, 0:TC]),
            inc=bank.free_s)
    else:
        k = j - DIT
        c.wait("scalar", c.sV, c.kt_free_v[k])
        c.op("scalar", lambda E, k=k, bank=bank, siluz=siluz: E.activation(
            siluz[:, k * TC:(k + 1) * TC], bank.ps[:, 0:TC], ACT.Silu),
            inc=bank.free_s)
    c.op("scalar", lambda E: E.nop(), inc=c.sS)
    c.xz_done_s = c.sS.n


def _prod_u_conv(c, ch, par, k):
    upf = c.upf2[par]
    cpar = k % 2
    cb = c.convd_buf[cpar]
    dc = c.dC0 if cpar == 0 else c.dC1
    c.wait("sync", c.sT, c.convd_free_t[cpar])
    c.op("sync", lambda E, k=k, cb=cb: E.dma_start(
        cb.rearrange("p (a q) -> p a q", a=4),
        bass.AP(c.t_convd, k * 512, [[64 * 128, 128], [128, 4], [1, 128]])),
        inc=dc, k=16)
    cslab = dc.n
    bank = c.banks[4 + (k % 2)]
    base = k * (TC + 4)
    c.wait_free("tensor", bank)
    c.wait("tensor", c.sS, c.xz_done_s)
    c.wait("tensor", dc, cslab)
    for tap in range(4):
        c.op("tensor", lambda E, tap=tap, cb=cb, bank=bank, base=base: E.matmul(
            bank.ps[:, 0:TC], cb[:, tap * 128:(tap + 1) * 128],
            c.uf[:, base + 1 + tap: base + 1 + tap + TC],
            start=(tap == 0), stop=(tap == 3)),
            inc=(bank.ready if tap == 3 else None))
    c.op("tensor", lambda E: E.nop(), inc=c.sT)
    c.convd_free_t[cpar] = c.sT.n
    c.wait("scalar", bank.ready)
    c.wait("scalar", c.sV, c.kt_free_v[k])
    c.op("scalar", lambda E, k=k, bank=bank, upf=upf: E.activation(
        upf[:, k * TC:(k + 1) * TC], bank.ps[:, 0:TC], ACT.Silu,
        bias=c.convb[:, k:k + 1]), inc=bank.free_s)
    c.op("scalar", lambda E, base=base: E.copy(
        c.uf[:, base + 1:base + 4], c.uf[:, base + 1 + TC:base + 4 + TC]),
        inc=c.sS)
    c.up_done_s = c.sS.n
    c.ubf_free_s = c.sS.n


def _prod_u_xdbl(c, ch, par):
    upf = c.upf2[par]
    bx = c.banks[4]
    c.wait_free("tensor", bx)
    c.wait("tensor", c.sS, c.up_done_s)
    for kt in range(DIT):
        c.op("tensor", lambda E, kt=kt, upf=upf: E.matmul(
            bx.ps[:, 0:TC], c.wxp_sb[:, kt * 128:(kt + 1) * 128],
            upf[:, kt * TC:(kt + 1) * TC],
            start=(kt == 0), stop=(kt == DIT - 1)),
            inc=(bx.ready if kt == DIT - 1 else None))
    c.wait("scalar", bx.ready)
    c.op("scalar", lambda E, par=par: E.copy(c.dt_bf2[par], bx.ps[64:128, 0:TC]))
    c.op("scalar", lambda E, par=par: E.copy(c.Bf2[par], bx.ps[0:16, 0:TC]))
    c.op("scalar", lambda E, par=par: E.copy(c.Cf2[par], bx.ps[32:48, 0:TC]),
         inc=bx.free_s)
    c.op("scalar", lambda E: E.nop(), inc=c.sS)
    c.xdbl_done_s[par] = c.sS.n


def _prod_u_bc(c, ch, par, n):
    Bbc, Cbc = c.Bbc2[par], c.Cbc2[par]
    xdbl_done = c.xdbl_done_s[par]
    for which, (sel, srcrow, dst) in enumerate(
            ((c.sel16B, c.Bf2[par], Bbc), (c.sel16C, c.Cf2[par], Cbc))):
        bank = c.banks[(2 * n + which) % 4]
        c.wait_free("tensor", bank)
        c.wait("tensor", c.sS, xdbl_done)
        c.op("tensor", lambda E, n=n, bank=bank, sel=sel, srcrow=srcrow: E.matmul(
            bank.ps[:, 0:TC], sel[:, n * 128:(n + 1) * 128], srcrow,
            start=True, stop=True), inc=bank.ready)
        c.wait("scalar", bank.ready)
        c.wait("scalar", c.sV, c.bc_free_v[par])
        c.op("scalar", lambda E, n=n, bank=bank, dst=dst: E.copy(
            dst[:, n * TC:(n + 1) * TC], bank.ps[:, 0:TC]), inc=bank.free_s)
    c.op("scalar", lambda E: E.nop(), inc=c.sS)
    c.bc_done_s[par] = c.sS.n


def _prod_u_delta(c, ch, par, kt):
    delta = c.delta2[par]
    dsl = slice(kt * TC, (kt + 1) * TC)
    tp2 = kt % 2
    bank = c.banks[4 + (kt % 2)]
    c.wait_free("tensor", bank)
    c.wait("tensor", c.sS, c.xdbl_done_s[par])
    c.op("tensor", lambda E, kt=kt, par=par, bank=bank: E.matmul(
        bank.ps[:, 0:TC], c.wdt_sb[:, kt * 128:(kt + 1) * 128], c.dt_bf2[par],
        start=True, stop=True), inc=bank.ready)
    c.wait("scalar", bank.ready)
    c.wait("scalar", c.sV, c.tsb_free_v[tp2])
    c.op("scalar", lambda E, kt=kt, bank=bank, tp2=tp2: E.activation(
        c.tsb1[tp2], bank.ps[:, 0:TC], ACT.Abs, bias=c.bdt[:, kt:kt + 1]))
    c.op("scalar", lambda E, kt=kt, bank=bank, tp2=tp2: E.activation(
        c.tsb2[tp2], bank.ps[:, 0:TC], ACT.Relu, bias=c.bdt[:, kt:kt + 1]),
        inc=bank.free_s)
    c.op("scalar", lambda E, tp2=tp2: E.activation(
        c.tsb1[tp2], c.tsb1[tp2], ACT.Exp, scale=-1.0))
    c.op("scalar", lambda E, tp2=tp2: E.activation(
        c.tsb1[tp2], c.tsb1[tp2], ACT.Ln, bias=1.0), inc=c.sS)
    sp_done = c.sS.n
    c.wait("vector", c.sS, sp_done)
    c.op("vector", lambda E, dsl=dsl, tp2=tp2, delta=delta: E.tensor_add(
        delta[:, dsl], c.tsb1[tp2], c.tsb2[tp2]), inc=c.sV)
    c.tsb_free_v[tp2] = c.sV.n
    c.delta_done_v[par] = c.sV.n


def prod_units(c, ch):
    par = ch % 2
    units = [lambda: _emit_prod_ln(c, ch, par)]
    for j in range(ET):
        units.append(lambda j=j: _prod_u_xz(c, ch, par, j))
    for k in range(DIT):
        units.append(lambda k=k: _prod_u_conv(c, ch, par, k))
    units.append(lambda: _prod_u_xdbl(c, ch, par))
    for n in range(DS):
        units.append(lambda n=n: _prod_u_bc(c, ch, par, n))
    for kt in range(DIT):
        units.append(lambda kt=kt: _prod_u_delta(c, ch, par, kt))
    return units


def emit_mamba_scan(c, ch, units=None):
    par = ch % 2
    units = units or []

    def vop(fn, inc=None):
        c.op("vector", fn, inc=inc)

    upf = c.upf2[par]
    siluz = c.siluz2[par]
    Bbc, Cbc = c.Bbc2[par], c.Cbc2[par]
    xpn = c.xpn2[par]
    delta = c.delta2[par]

    per_kt = (len(units) + DIT - 1) // DIT if units else 0

    for kt in range(DIT):
        dsl = slice(kt * TC, (kt + 1) * TC)
        vop(lambda E, dsl=dsl, upf=upf, delta=delta: E.tensor_tensor(
            c.du, delta[:, dsl], upf[:, dsl], OP.mult), inc=c.sV)
        du_done = c.sV.n
        c.wait("scalar", c.sV, du_done)
        c.wait("scalar", c.sV, c.delta_done_v[par])
        c.wait("vector", c.sS, c.bc_done_s[par])

        yb = c.banks[6 + (kt % 2)]
        for n in range(DS):
            pp = n % 4
            hp = n % 2
            col = kt * DS + n
            c.wait("scalar", c.sV, c.dA_free_v[pp])
            c.op("scalar", lambda E, dsl=dsl, col=col, pp=pp, delta=delta: E.activation(
                c.dA[pp], delta[:, dsl], ACT.Exp,
                scale=c.A_sb[:, col:col + 1]), inc=c.sS)
            dA_ready = c.sS.n
            c.wait("vector", c.sS, dA_ready)
            vop(lambda E, n=n, pp=pp, Bbc=Bbc: E.tensor_tensor(
                c.beta[pp % 2], c.du, Bbc[:, n * TC:(n + 1) * TC], OP.mult))
            vop(lambda E, col=col, pp=pp: E.tensor_tensor_scan(
                c.h[pp % 2], c.dA[pp], c.beta[pp % 2],
                c.carry[:, col:col + 1], OP.mult, OP.add), inc=c.sV)
            c.dA_free_v[pp] = c.sV.n
            vop(lambda E, col=col, pp=pp: E.tensor_copy(
                c.carry[:, col:col + 1], c.h[pp % 2][:, TC - 1:TC]))
            c.wait("vector", c.sT, c.hC_free_t[hp])
            vop(lambda E, n=n, pp=pp, hp=hp, Cbc=Cbc: E.tensor_tensor(
                c.hC[hp], c.h[pp % 2], Cbc[:, n * TC:(n + 1) * TC], OP.mult),
                inc=c.sV)
            hC_ready = c.sV.n
            if n == 0:
                c.wait_free("tensor", yb)
            c.wait("tensor", c.sV, hC_ready)
            c.op("tensor", lambda E, hp=hp, yb=yb, n=n: E.matmul(
                yb.ps[:, 0:TC], c.ident_bf, c.hC[hp],
                start=(n == 0), stop=(n == DS - 1)),
                inc=(yb.ready if n == DS - 1 else None))
            c.op("tensor", lambda E: E.nop(), inc=c.sT)
            c.hC_free_t[hp] = c.sT.n

        c.wait("vector", yb.ready)
        vop(lambda E, dsl=dsl, kt=kt, yb=yb, upf=upf: E.scalar_tensor_tensor(
            c.tmp1, upf[:, dsl], c.Dsk[:, kt:kt + 1], yb.ps[:, 0:TC],
            OP.mult, OP.add), inc=yb.free_v)
        vop(lambda E, dsl=dsl, siluz=siluz: E.tensor_tensor(
            c.yg[:, dsl], c.tmp1, siluz[:, dsl], OP.mult), inc=c.sV)
        c.kt_free_v[kt] = c.sV.n
        if units:
            for u in units[kt * per_kt:(kt + 1) * per_kt]:
                u()
    if units:
        for u in units[DIT * per_kt:]:
            u()
    yg_done = c.sV.n
    c.bc_free_v[par] = yg_done

    for j in range(DMT):
        wpar = j % 2
        wb = c.wout_buf[wpar]
        du_sem = c.dU0 if wpar == 0 else c.dU1
        c.wait("sync", c.sT, c.wout_free_t[wpar])
        c.op("sync", lambda E, j=j, wb=wb: E.dma_start(
            wb.rearrange("p (k q) -> p k q", k=DIT),
            bass.AP(c.d_woutbf, j * 128, [[1024, 128], [128 * 1024, DIT], [1, 128]])),
            inc=du_sem, k=16)
        wout_slab = du_sem.n
        bank = c.banks[4 + (j % 2)]
        c.wait_free("tensor", bank)
        c.wait("tensor", c.sV, yg_done)
        c.wait("tensor", du_sem, wout_slab)
        for kt in range(DIT):
            c.op("tensor", lambda E, kt=kt, wb=wb, bank=bank: E.matmul(
                bank.ps[:, 0:TC], wb[:, kt * 128:(kt + 1) * 128],
                c.yg[:, kt * TC:(kt + 1) * TC],
                start=(kt == 0), stop=(kt == DIT - 1)),
                inc=(bank.ready if kt == DIT - 1 else None))
        c.op("tensor", lambda E: E.nop(), inc=c.sT)
        c.wout_free_t[wpar] = c.sT.n
        c.wait("vector", bank.ready)
        vop(lambda E, j=j, bank=bank, xpn=xpn: E.tensor_tensor(
            xpn[:, j * TC:(j + 1) * TC], xpn[:, j * TC:(j + 1) * TC],
            bank.ps[:, 0:TC], OP.add), inc=bank.free_v)
    vop(lambda E: E.nop(), inc=c.sV)
    x1_done = c.sV.n

    c.wait("sync", c.sV, x1_done)
    c.sync_dma(lambda E, ch=ch, xpn=xpn: E.dma_start(
        bass.AP(c.d_agin[ch], 0, [[TC, 128], [128 * TC, DMT], [1, TC]]),
        xpn.rearrange("p (k t) -> p k t", k=DMT)))
    staged = c.dL.n
    c.xpn_free_dl[par] = staged
    if c.debug_x1 is not None:
        c.sync_dma(lambda E, ch=ch, xpn=xpn: E.dma_start(
            bass.AP(c.debug_x1, ch * TC, [[L, 128], [128 * L, DMT], [1, TC]]),
            xpn.rearrange("p (k t) -> p k t", k=DMT)))
        c.xpn_free_dl[par] = c.dL.n
    c.wait("gpsimd", c.dL, staged)
    c.op("gpsimd", lambda E, ch=ch: E.collective_compute(
        "AllGather", OP.bypass, [list(range(NCORES))],
        [c.d_agin[ch].ap()], [c.d_agout[ch].ap()]), inc=c.CC)
    c.wait("scalar", c.CC, c.CC.n)
    nelem = NCORES * DM * TC
    c.op("scalar", lambda E, ch=ch, nelem=nelem: E.dma_start(
        bass.AP(c.d_bigag, ch * nelem, [[65536, nelem // 65536], [1, 65536]]),
        bass.AP(c.d_agout[ch], 0, [[65536, nelem // 65536], [1, 65536]])),
        inc=c.dSc, k=16, dma=True)



def _xtile_wait(c, eng, par):
    c.wait(eng, c.sT, c.xtile_free_t[par])
    c.wait(eng, c.sV, c.xtile_free_v[par])
    if c.xtile_store_w[par] is not None:
        c.xtile_store_w[par](eng)


def emit_ln_stream(c, load_tile, apply_dst, nk, extra_per_tile=None):
    r = c.rows_f
    bS, bS2 = c.banks[0], c.banks[1]
    c.wait_free("tensor", bS)
    c.wait_free("tensor", bS2)
    for kt in range(nk):
        par = kt % 2
        w = load_tile(kt, par)
        w("scalar")
        c.wait("scalar", c.sT, c.xtile_free_t[par])
        c.op("scalar", lambda E, par=par: E.activation(
            c.xsqt[par], c.xtile[par], ACT.Square), inc=c.sS)
        sq = c.sS.n
        w("tensor")
        c.op("tensor", lambda E, par=par, kt=kt, nk=nk: E.matmul(
            bS.ps[0:1, 0:TF], c.ones_col, c.xtile[par],
            start=(kt == 0), stop=(kt == nk - 1)),
            inc=(bS.ready if kt == nk - 1 else None))
        c.wait("tensor", c.sS, sq)
        c.op("tensor", lambda E, par=par, kt=kt, nk=nk: E.matmul(
            bS2.ps[0:1, 0:TF], c.ones_col_bf, c.xsqt[par],
            start=(kt == 0), stop=(kt == nk - 1)),
            inc=(bS2.ready if kt == nk - 1 else None))
        c.op("tensor", lambda E: E.nop(), inc=c.sT)
        c.xtile_free_t[par] = c.sT.n
    c.wait("vector", bS.ready)
    c.wait("vector", bS2.ready)
    c.op("vector", lambda E: E.tensor_scalar(r["mu"], bS.ps[0:1, 0:TF],
                                             1.0 / DM, None, OP.mult), inc=bS.free_v)
    c.op("vector", lambda E: E.tensor_scalar(r["ex2"], bS2.ps[0:1, 0:TF],
                                             1.0 / DM, None, OP.mult), inc=bS2.free_v)
    c.op("vector", lambda E: E.tensor_tensor(r["mu2"], r["mu"], r["mu"], OP.mult))
    c.op("vector", lambda E: E.tensor_tensor(r["var"], r["ex2"], r["mu2"],
                                             OP.subtract))
    c.op("vector", lambda E: E.tensor_scalar(r["veps"], r["var"], EPS, None, OP.add))
    c.op("vector", lambda E: E.reciprocal(r["inv"], r["veps"]), inc=c.sV)
    c.wait("scalar", c.sV, c.sV.n)
    c.op("scalar", lambda E: E.activation(r["r1"], r["inv"], ACT.Sqrt), inc=c.sS)
    c.wait("vector", c.sS, c.sS.n)
    c.op("vector", lambda E: E.tensor_tensor(r["mur"], r["mu"], r["r1"], OP.mult),
         inc=c.sV)
    rows_done = c.sV.n
    bA, bB = c.banks[2], c.banks[3]
    for bank, row in ((bA, r["r1"]), (bB, r["mur"])):
        c.wait_free("tensor", bank)
        c.wait("tensor", c.sV, rows_done)
        c.op("tensor", lambda E, bank=bank, row=row: E.matmul(
            bank.ps[:, 0:TF], c.ones_row, row, start=True, stop=True),
            inc=bank.ready)
    c.wait("vector", bA.ready)
    c.wait("vector", bB.ready)
    for kt in range(nk):
        par = kt % 2
        w = load_tile(kt, par)
        w("vector")
        c.wait("vector", c.sT, c.xtile_free_t[par])
        c.op("vector", lambda E, par=par: E.scalar_tensor_tensor(
            c.xnt, c.xtile[par], 1.0, bA.ps[:, 0:TF], OP.mult, OP.mult))
        c.op("vector", lambda E, par=par, kt=kt: E.tensor_tensor(
            apply_dst(kt), c.xnt, bB.ps[:, 0:TF], OP.subtract), inc=c.sV)
        c.xtile_free_v[par] = c.sV.n
        if extra_per_tile is not None:
            extra_per_tile(kt)
    c.op("vector", lambda E: E.nop(), inc=bA.free_v)
    c.op("vector", lambda E: E.nop(), inc=bB.free_v)
    c.op("vector", lambda E: E.nop(), inc=c.sV)
    return c.sV.n


def mk_gather_loader(c, t):
    def load_tile(kt, par):
        jcol0 = (t * DMT + kt) * 2
        _xtile_wait(c, "gpsimd", par)
        ws = []
        for h2 in range(2):
            ws.append(c.pipe_dma("gpsimd", lambda E, par=par, jc=jcol0 + h2,
                                 h2=h2: E.indirect_dma_start(
                c.xtile[par][:, h2 * TC:(h2 + 1) * TC], None, c.d_bigag.ap(),
                IndirectOffsetOnAxis(c.gidx[:, jc:jc + 1], 0))))

        def w(eng, ws=ws):
            for wi in ws:
                wi(eng)
        return w
    return load_tile


def mk_dram_loader(c, dram, t, store_w=None):
    def load_tile(kt, par):
        _xtile_wait(c, "sync", par)
        if store_w is not None and store_w[t][kt] is not None:
            store_w[t][kt]("sync")
        return c.pipe_dma("sync", lambda E, kt=kt, par=par: E.dma_start(
            c.xtile[par],
            bass.AP(dram, t * DM * TF + kt * 128 * TF, [[TF, 128], [1, TF]])))
    return load_tile



def emit_ffn(c, resid_loader, out_writer):
    c.wait("sync", c.sV, c.c0_free_v)
    c.sync_dma(lambda E: E.dma_start(
        c.c_sb.rearrange("p (k t) -> p k t", k=DMT),
        bass.AP(c.d_c0, 0, [[TF, 128], [128 * TF, DMT], [1, TF]])))
    c0_loaded = c.dL.n
    c.wait("vector", c.dL, c0_loaded)

    for t in range(B):
        c.wait("vector", c.sT, c.cbf_free_t)
        c.op("vector", lambda E: E.tensor_copy(c.cbf, c.c_sb), inc=c.sV)
        cbf_done = c.sV.n
        c.wait("sync", c.sV, c.xf_free_v)
        if c.xf_store_w[t] is not None:
            c.xf_store_w[t]("sync")
        xfw = c.pipe_dma("sync", lambda E, t=t: E.dma_start(
            c.xf_sb.rearrange("p (k t2) -> p k t2", k=DMT),
            bass.AP(c.d_xf, t * DM * TF, [[TF, 128], [128 * TF, DMT], [1, TF]])))
        for j in range(DMT):
            for gi, g in enumerate(("f", "i", "g")):
                bank = c.banks[4 + gi]
                wsb, usb = c.gate_w[g]
                c.wait_free("tensor", bank)
                xfw("tensor")
                c.wait("tensor", c.sV, cbf_done)
                for kt in range(DMT):
                    c.op("tensor", lambda E, j=j, kt=kt, wsb=wsb, bank=bank: E.matmul(
                        bank.ps[:, 0:TF],
                        wsb[:, kt * 1024 + j * 128: kt * 1024 + (j + 1) * 128],
                        c.xf_sb[:, kt * TF:(kt + 1) * TF],
                        start=(kt == 0), stop=False))
                for kt in range(DMT):
                    c.op("tensor", lambda E, j=j, kt=kt, usb=usb, bank=bank: E.matmul(
                        bank.ps[:, 0:TF],
                        usb[:, kt * 1024 + j * 128: kt * 1024 + (j + 1) * 128],
                        c.cbf[:, kt * TF:(kt + 1) * TF],
                        start=False, stop=(kt == DMT - 1)),
                        inc=(bank.ready if kt == DMT - 1 else None))
                fn = ACT.Sigmoid if g in ("f", "i") else ACT.Tanh
                bias = {"f": c.bf_c, "i": c.bi_c, "g": c.bc_c}[g]
                c.wait("scalar", bank.ready)
                c.op("scalar", lambda E, g=g, j=j, bank=bank, fn=fn, bias=bias:
                     E.activation(c.gbuf[g], bank.ps[:, 0:TF], fn,
                                  bias=bias[:, j:j + 1]), inc=c.sS)
            gates_done = c.sS.n
            csl = slice(j * TF, (j + 1) * TF)
            c.wait("vector", c.sS, gates_done)
            c.op("vector", lambda E: E.tensor_tensor(
                c.tmpf, c.gbuf["i"], c.gbuf["g"], OP.mult))
            c.op("vector", lambda E, csl=csl: E.tensor_tensor(
                c.tmpf2, c.gbuf["f"], c.c_sb[:, csl], OP.mult))
            for gi in range(3):
                c.op("vector", lambda E: E.nop(), inc=c.banks[4 + gi].free_v)
            c.op("vector", lambda E, csl=csl: E.tensor_tensor(
                c.c_sb[:, csl], c.tmpf, c.tmpf2, OP.add), inc=c.sV)
        c.op("vector", lambda E: E.nop(), inc=c.sV)
        c.xf_free_v = c.sV.n
        c.op("tensor", lambda E: E.nop(), inc=c.sT)
        c.cbf_free_t = c.sT.n

    c.wait("vector", c.sT, c.cbf_free_t)
    c.op("vector", lambda E: E.tensor_copy(c.cbf, c.c_sb), inc=c.sV)
    cbf_fin = c.sV.n
    for j in range(DMT):
        bank = c.banks[4 + (j % 2)]
        c.wait_free("tensor", bank)
        c.wait("tensor", c.sV, cbf_fin)
        for wz in c.wouo_w:
            wz("tensor")
        for kt in range(DMT):
            c.op("tensor", lambda E, j=j, kt=kt, bank=bank: E.matmul(
                bank.ps[:, 0:TF],
                c.uo_sb[:, kt * 1024 + j * 128: kt * 1024 + (j + 1) * 128],
                c.cbf[:, kt * TF:(kt + 1) * TF],
                start=(kt == 0), stop=(kt == DMT - 1)),
                inc=(bank.ready if kt == DMT - 1 else None))
        c.wait("scalar", bank.ready)
        c.op("scalar", lambda E, j=j, bank=bank: E.copy(
            c.cuo_bf[:, j * TF:(j + 1) * TF], bank.ps[:, 0:TF]), inc=bank.free_s)
    c.op("scalar", lambda E: E.nop(), inc=c.sS)
    cuo_done = c.sS.n

    for t in range(B):
        c.wait("sync", c.sV, c.xf_free_v)
        if c.xf_store_w[t] is not None:
            c.xf_store_w[t]("sync")
        xfw = c.pipe_dma("sync", lambda E, t=t: E.dma_start(
            c.xf_sb.rearrange("p (k t2) -> p k t2", k=DMT),
            bass.AP(c.d_xf, t * DM * TF, [[TF, 128], [128 * TF, DMT], [1, TF]])))
        for j in range(DMT):
            bank = c.banks[4 + (j % 2)]
            c.wait_free("tensor", bank)
            xfw("tensor")
            c.wait("tensor", c.sS, cuo_done)
            for kt in range(DMT):
                c.op("tensor", lambda E, j=j, kt=kt, bank=bank: E.matmul(
                    bank.ps[:, 0:TF],
                    c.wo_sb[:, kt * 1024 + j * 128: kt * 1024 + (j + 1) * 128],
                    c.xf_sb[:, kt * TF:(kt + 1) * TF],
                    start=(kt == 0), stop=False))
            c.op("tensor", lambda E, j=j, bank=bank: E.matmul(
                bank.ps[:, 0:TF], c.ident_bf, c.cuo_bf[:, j * TF:(j + 1) * TF],
                start=False, stop=True), inc=bank.ready)
            c.wait("scalar", bank.ready)
            c.wait("scalar", c.sV, c.hbuf_free_v)
            c.op("scalar", lambda E, j=j, bank=bank: E.activation(
                c.hbuf, bank.ps[:, 0:TF], ACT.Tanh, bias=c.bo_c[:, j:j + 1]),
                inc=c.sS)
            h_done = c.sS.n
            pi = (t * DMT + j) % 2
            res_w = resid_loader(t, j)
            c.wait("vector", c.sS, h_done)
            res_w("vector")
            if c.obuf_w[pi] is not None:
                c.obuf_w[pi]("vector")
            c.op("vector", lambda E, pi=pi: E.tensor_tensor(
                c.obuf[pi], c.res_buf[pi], c.hbuf, OP.add), inc=c.sV)
            c.hbuf_free_v = c.sV.n
            c.res_free_v[pi] = c.sV.n
            c.op("vector", lambda E, bank=bank: E.nop(), inc=bank.free_v)
            out_writer(t, j, c.sV.n)
        c.op("vector", lambda E: E.nop(), inc=c.sV)
        c.xf_free_v = c.sV.n
    c.op("tensor", lambda E: E.nop(), inc=c.sT)



def emit_phase2(c, out):
    mam = dict(sV=c.sV.n, sS=c.sS.n, sT=c.sT.n, dL=c.dL.n)
    for eng in ENGINES:
        c.wait(eng, c.CC, NCH)
        c.wait(eng, c.dSc, 16 * NCH)
        for nm in ("sV", "sS", "sT", "dL"):
            c.wait(eng, getattr(c, nm), mam[nm])
    c.op("vector", lambda E: E.memset(c.ones_tf, 1.0))

    def load_w(dst, dram):
        return c.pipe_dma("sync", lambda E, dst=dst, dram=dram: E.dma_start(
            dst.rearrange("p (k q) -> p k q", k=DMT),
            bass.AP(dram, 0, [[1024, 128], [128 * 1024, DMT], [1, 1024]])))

    for g, wn, un in (("f", "wfT", "ufT"), ("i", "wiT", "uiT"), ("g", "wcT", "ucT")):
        load_w(c.gate_w[g][0], c.ffn_bf[wn])
        load_w(c.gate_w[g][1], c.ffn_bf[un])
    c.wouo_w = [load_w(c.wo_sb, c.ffn_bf["woT"]),
                load_w(c.uo_sb, c.ffn_bf["uoT"])]
    c.pipe_drain("tensor")

    for t in range(B):
        if t > 0:
            c.xf_store_w[t - 1]("vector")
        done = emit_ln_stream(
            c, mk_gather_loader(c, t),
            lambda kt: c.xf_sb[:, kt * TF:(kt + 1) * TF], DMT)
        c.wait("sync", c.sV, done)
        c.xf_store_w[t] = c.pipe_dma("sync", lambda E, t=t: E.dma_start(
            bass.AP(c.d_xf, t * DM * TF, [[TF, 128], [128 * TF, DMT], [1, TF]]),
            c.xf_sb.rearrange("p (k t2) -> p k t2", k=DMT)))
        c.op("vector", lambda E: E.nop(), inc=c.sV)
    c.xf_free_v = c.sV.n

    def resid1(t, j):
        jcol0 = (t * DMT + j) * 2
        pi = (t * DMT + j) % 2
        c.wait("gpsimd", c.sV, c.res_free_v[pi])
        ws = []
        for h2 in range(2):
            ws.append(c.pipe_dma("gpsimd", lambda E, jc=jcol0 + h2, h2=h2,
                                 pi=pi: E.indirect_dma_start(
                c.res_buf[pi][:, h2 * TC:(h2 + 1) * TC], None, c.d_bigag.ap(),
                IndirectOffsetOnAxis(c.gidx[:, jc:jc + 1], 0))))

        def w(eng, ws=ws):
            for wi in ws:
                wi(eng)
        return w

    def write_x2(t, j, v_done):
        pi = (t * DMT + j) % 2
        c.wait("sync", c.sV, v_done)
        c.obuf_w[pi] = c.pipe_dma("sync", lambda E, t=t, j=j, pi=pi: E.dma_start(
            bass.AP(c.d_x2, t * DM * TF + j * 128 * TF, [[TF, 128], [1, TF]]),
            c.obuf[pi]))
        c.x2_store_w[t][j] = c.obuf_w[pi]

    emit_ffn(c, resid1, write_x2)

    for t in range(B):
        def extra(kt, t=t):
            c.op("vector", lambda E, kt=kt, t=t: E.tensor_reduce(
                c.vsumP[:, kt * B + t: kt * B + t + 1], c.xnt, AX.X, OP.add))
        done = emit_ln_stream(
            c, mk_dram_loader(c, c.d_x2, t, store_w=c.x2_store_w),
            lambda kt: c.xnt, DMT, extra_per_tile=extra)
    vsum_done = done
    c.wait("sync", c.sV, vsum_done)
    c.sync_dma(lambda E: E.dma_start(c.d_ar_in.ap(), c.vsumP))
    c.wait("gpsimd", c.dL, c.dL.n)
    c.op("gpsimd", lambda E: E.collective_compute(
        "AllReduce", OP.add, [list(range(NCORES))],
        [c.d_ar_in.ap()], [c.d_ar_out.ap()]), inc=c.CC)
    c.wait("sync", c.CC, c.CC.n)
    c.sync_dma(lambda E: E.dma_start(c.vsumP, c.d_ar_out.ap()))
    vs_back = c.dL.n
    c.wait("sync", c.sT, c.sT.n)
    wvw = [load_w(c.wv_sb, c.ffn_bf["wvT"]),
           load_w(c.woa_sb, c.ffn_bf["woaT"])]
    c.wait("vector", c.dL, vs_back)
    c.op("vector", lambda E: E.tensor_copy(c.vsum_bf, c.vsumP), inc=c.sV)
    vbf_done = c.sV.n
    bank = c.banks[6]
    c.wait("tensor", c.sV, vbf_done)
    for wz in wvw:
        wz("tensor")
    for j in range(DMT):
        c.wait_free("tensor", bank)
        for kt in range(DMT):
            c.op("tensor", lambda E, j=j, kt=kt: E.matmul(
                bank.ps[:, 0:B],
                c.wv_sb[:, kt * 1024 + j * 128: kt * 1024 + (j + 1) * 128],
                c.vsum_bf[:, kt * B:(kt + 1) * B],
                start=(kt == 0), stop=(kt == DMT - 1)),
                inc=(bank.ready if kt == DMT - 1 else None))
        c.wait("scalar", bank.ready)
        c.op("scalar", lambda E, j=j: E.copy(
            c.av_bf[:, j * B:(j + 1) * B], bank.ps[:, 0:B]), inc=bank.free_s)
    c.op("scalar", lambda E: E.nop(), inc=c.sS)
    av_done = c.sS.n
    c.wait("tensor", c.sS, av_done)
    for j in range(DMT):
        c.wait_free("tensor", bank)
        for kt in range(DMT):
            c.op("tensor", lambda E, j=j, kt=kt: E.matmul(
                bank.ps[:, 0:B],
                c.woa_sb[:, kt * 1024 + j * 128: kt * 1024 + (j + 1) * 128],
                c.av_bf[:, kt * B:(kt + 1) * B],
                start=(kt == 0), stop=(kt == DMT - 1)),
                inc=(bank.ready if kt == DMT - 1 else None))
        c.wait("scalar", bank.ready)
        c.op("scalar", lambda E, j=j: E.copy(
            c.attnv[:, j * B:(j + 1) * B], bank.ps[:, 0:B]), inc=bank.free_s)
    c.op("scalar", lambda E: E.nop(), inc=c.sS)
    attnv_done = c.sS.n
    c.op("tensor", lambda E: E.nop(), inc=c.sT)
    c.wait("sync", c.sT, c.sT.n)
    c.wouo_w = [load_w(c.wo_sb, c.ffn_bf["woT"]),
                load_w(c.uo_sb, c.ffn_bf["uoT"])]

    c.wait("vector", c.sS, attnv_done)
    c.wait("vector", c.dL, c.dL.n)
    for t in range(B):
        for kt in range(DMT):
            par = kt % 2
            _xtile_wait(c, "sync", par)
            if c.x2_store_w[t][kt] is not None:
                c.x2_store_w[t][kt]("sync")
            x2w = c.pipe_dma("sync", lambda E, t=t, kt=kt, par=par: E.dma_start(
                c.xtile[par],
                bass.AP(c.d_x2, t * DM * TF + kt * 128 * TF, [[TF, 128], [1, TF]])))
            x2w("vector")
            c.op("vector", lambda E, kt=kt, t=t, par=par: E.scalar_tensor_tensor(
                c.xtile[par], c.ones_tf, c.attnv[:, kt * B + t: kt * B + t + 1],
                c.xtile[par], OP.mult, OP.add), inc=c.sV)
            x3v = c.sV.n
            c.wait("sync", c.sV, x3v)
            sw = c.pipe_dma("sync", lambda E, t=t, kt=kt, par=par: E.dma_start(
                bass.AP(c.d_x3, t * DM * TF + kt * 128 * TF, [[TF, 128], [1, TF]]),
                c.xtile[par]))
            c.x3_store_w[t][kt] = sw
            c.xtile_store_w[par] = sw
            c.xtile_free_v[kt % 2] = c.sV.n
        if t > 0:
            c.xf_store_w[t - 1]("vector")
        done = emit_ln_stream(
            c, mk_dram_loader(c, c.d_x3, t, store_w=c.x3_store_w),
            lambda kt: c.xf_sb[:, kt * TF:(kt + 1) * TF], DMT)
        c.wait("sync", c.sV, done)
        c.xf_store_w[t] = c.pipe_dma("sync", lambda E, t=t: E.dma_start(
            bass.AP(c.d_xf, t * DM * TF, [[TF, 128], [128 * TF, DMT], [1, TF]]),
            c.xf_sb.rearrange("p (k t2) -> p k t2", k=DMT)))
        c.op("vector", lambda E: E.nop(), inc=c.sV)
        c.xf_free_v = c.sV.n

    c.c0_free_v = c.sV.n

    def resid2(t, j):
        pi = (t * DMT + j) % 2
        c.wait("sync", c.sV, c.res_free_v[pi])
        if c.x3_store_w[t][j] is not None:
            c.x3_store_w[t][j]("sync")
        return c.pipe_dma("sync", lambda E, t=t, j=j, pi=pi: E.dma_start(
            c.res_buf[pi],
            bass.AP(c.d_x3, t * DM * TF + j * 128 * TF, [[TF, 128], [1, TF]])))

    def write_out(t, j, v_done):
        pi = (t * DMT + j) % 2
        c.wait("sync", c.sV, v_done)
        c.obuf_w[pi] = c.pipe_dma("sync", lambda E, t=t, j=j, pi=pi: E.dma_start(
            bass.AP(out, t * DM * TF + j * 128 * TF, [[TF, 128], [1, TF]]),
            c.obuf[pi]))

    emit_ffn(c, resid2, write_out)



def build(debug=None):
    nc = bass.Bass("TRN2", target_bir_lowering=False)
    nc.detect_race_conditions = False
    c = C(nc)
    t = {}

    def inp(name, shape, dt=F32):
        t[name] = nc.dram_tensor(name, shape, dt, kind="ExternalInput")
        return t[name]

    c.t_x = inp("xT", [DM, L])
    c.t_win = inp("w_in", [DM, 2 * DI])
    c.t_wxp = inp("w_xp", [DI, 128])
    c.t_wdt = inp("w_dt", [RK, DI])
    c.t_wout = inp("w_out", [DI, DM])
    c.t_convd = inp("convd", [128, 64 * 128])
    c.t_alog = inp("alog", [128, 256])
    c.t_gamma = inp("gamma8", [128, DMT])
    c.t_beta = inp("beta8", [128, DMT])
    c.t_convb = inp("convb16", [128, DIT])
    c.t_bdt = inp("bdt16", [128, DIT])
    c.t_dsk = inp("dsk16", [128, DIT])
    c.t_bf = inp("bf8", [128, DMT])
    c.t_bi = inp("bi8", [128, DMT])
    c.t_bc = inp("bc8", [128, DMT])
    c.t_bo = inp("bo8", [128, DMT])
    c.t_sel16 = inp("sel16", [16, 16 * 128])
    c.t_ident = inp("ident", [128, 128])
    c.t_gidx = inp("gidx", [128, 128], I32)
    c.t_c0 = inp("c0", [DM, TF])
    ffn_names = ["wfT", "ufT", "wiT", "uiT", "wcT", "ucT", "woT", "uoT",
                 "wvT", "woaT"]
    for nm in ffn_names:
        inp(nm, [DM, DM])

    out = nc.dram_tensor("out", [B, DM, TF], F32, kind="ExternalOutput")
    c.debug_x1 = None
    c.dbg = {}
    c.strong_barrier = True
    if debug == "mamba":
        c.debug_x1 = nc.dram_tensor("dbg_x1", [DM, L], F32, kind="ExternalOutput")
        for nm, shp, dt2 in [("up", [DI, L], F32), ("sz", [DI, L], F32),
                             ("dl", [DI, L], F32), ("yg", [DI, L], BF16),
                             ("bc", [128, L], F32), ("xn2", [DM, L], BF16),
                             ("ubf", [128, NCH * DIT * (TC + 4)], F32)]:
            c.dbg[nm] = nc.dram_tensor("dbg_" + nm, shp, dt2, kind="ExternalOutput")

    c.d_winbf = nc.dram_tensor("winbf", [DM, 2 * DI], BF16, kind="Internal")
    c.d_woutbf = nc.dram_tensor("woutbf", [DI, DM], BF16, kind="Internal")
    c.ffn_bf = {nm: nc.dram_tensor(nm + "bf", [DM, DM], BF16, kind="Internal")
                for nm in ffn_names}
    c.ffn_casts = [(t[nm], c.ffn_bf[nm]) for nm in ffn_names]
    c.d_agin = [nc.dram_tensor(f"agin{i}", [DM, TC], F32, kind="Internal")
                for i in range(NCH)]
    c.d_agout = [nc.dram_tensor(f"agout{i}", [NCORES * DM, TC], F32, kind="Internal")
                 for i in range(NCH)]
    c.d_bigag = nc.dram_tensor("bigag", [NCH * NCORES * DM, TC], F32,
                               kind=("ExternalOutput" if debug == "full" else "Internal"))
    c.d_c0 = t["c0"]
    scratch_kind = "ExternalOutput" if debug == "full" else "Internal"
    c.d_xf = nc.dram_tensor("xfscr", [B * DM, TF], BF16, kind=scratch_kind)
    c.d_x2 = nc.dram_tensor("x2scr", [B * DM, TF], F32, kind=scratch_kind)
    c.d_x3 = nc.dram_tensor("x3scr", [B * DM, TF], F32, kind=scratch_kind)
    c.d_ar_in = nc.dram_tensor("arin", [128, 64], F32, kind="Internal")
    c.d_ar_out = nc.dram_tensor("arout", [128, 64], F32, kind="Internal")

    with ExitStack() as ctx:
        e = ctx.enter_context
        arena_t = e(nc.sbuf_tensor("arena", [128, 53000], F32))
        A = Arena(arena_t[:])
        for nm in ["dL", "dSc", "dG", "CC", "sV", "sS", "sT", "sG",
                   "dX", "dW0", "dW1", "dU0", "dU1", "dC0", "dC1"]:
            setattr(c, nm, Sem(e(nc.semaphore(nm))))
        c.pd = {"sync": [Sem(e(nc.semaphore(f"dP{i}"))) for i in range(4)],
                "gpsimd": [Sem(e(nc.semaphore(f"dQ{i}"))) for i in range(4)]}
        c.pd_i = {"sync": 0, "gpsimd": 0}
        c.banks = []
        for i in range(8):
            ps = e(nc.psum_tensor(f"P{i}", [128, 512], F32))
            c.banks.append(Bank(ps[:], Sem(e(nc.semaphore(f"bkr{i}"))),
                                Sem(e(nc.semaphore(f"bkfs{i}"))),
                                Sem(e(nc.semaphore(f"bkfv{i}")))))

        c.ones_col = A.f32(128, 1)
        c.ones_col_bf = A.bf16(128, 2)[:, 0:1]
        c.ones_row = A.f32(1, 128)
        c.A_sb = A.f32(128, 256)
        c.gam = A.f32(128, DMT)
        c.ngam = A.f32(128, DMT)
        c.bet = A.f32(128, DMT)
        c.convb = A.f32(128, DIT)
        c.bdt = A.f32(128, DIT)
        c.Dsk = A.f32(128, DIT)
        c.bf_c = A.f32(128, DMT)
        c.bi_c = A.f32(128, DMT)
        c.bc_c = A.f32(128, DMT)
        c.bo_c = A.f32(128, DMT)
        c.carry = A.f32(128, 256)
        c.ident_bf = A.bf16(128, 128)
        c.gidx = A.i32(128, 128)
        rows_all = A.f32(1, 16 * 256)
        c.rows = {}
        for i2, nm in enumerate(["mu", "ex2", "mu2", "var", "veps", "inv",
                                 "r1", "vfrac", "veps2", "inv2", "r2", "R",
                                 "muR", "onepr", "mur"]):
            c.rows[nm] = rows_all[:, i2 * 256:i2 * 256 + TC]
        c.rows_f = {}
        for i2, nm in enumerate(["mu", "ex2", "mu2", "var", "veps", "inv",
                                 "r1", "mur"]):
            c.rows_f[nm] = rows_all[:, i2 * TF:(i2 + 1) * TF]

        A.mark()
        c.wout_buf = [A.bf16(128, DIT * 128) for _ in range(2)]
        c.convd_buf = [A.f32(128, 512) for _ in range(2)]
        c.wxp_sb = A.bf16(128, DIT * 128)
        _wdt_reg = A.f32(128, DI // 2)
        c.wdt_sb = _wdt_reg.bitcast(BF16)[64:128, 0:DI]
        c.win_buf = [A.bf16(128, DMT * 128) for _ in range(2)]
        c.x_sb = A.f32(128, DMT * TC)
        c.xpn2 = [A.f32(128, DMT * TC) for _ in range(2)]
        c.xn2bf = A.bf16(128, DMT * TC)
        c.xsq_bf = A.bf16(128, DMT * TC)
        c.uf = A.f32(128, DIT * (TC + 4))
        c.upf2 = [A.bf16(128, DIT * TC) for _ in range(2)]
        c.siluz2 = [A.bf16(128, DIT * TC) for _ in range(2)]
        delta_reg = A.f32(128, DIT * TC)
        _d16 = delta_reg.bitcast(BF16)
        c.delta2 = [_d16[:, 0:DIT * TC], _d16[:, DIT * TC:]]
        c.du = A.f32(128, TC)
        c.yg = A.bf16(128, DIT * TC)
        c.tmp1 = A.f32(128, TC)
        c.tmp2 = A.f32(128, TC)
        c.tsb1 = [A.f32(128, TC) for _ in range(2)]
        c.tsb2 = [A.f32(128, TC) for _ in range(2)]
        _dt_reg = [A.f32(128, TC // 2) for _ in range(2)]
        c.dt_bf2 = [t.bitcast(BF16)[64:128, 0:TC] for t in _dt_reg]
        bc_strip = [A.f32(128, TC) for _ in range(2)]
        c.Bf2 = [t[0:16, :] for t in bc_strip]
        c.Cf2 = [t[32:48, :] for t in bc_strip]
        bcbc = A.f32(128, 2 * DS * TC)
        _bc16 = bcbc.bitcast(BF16)
        c.Bbc2 = [_bc16[:, 0:DS * TC], _bc16[:, DS * TC:2 * DS * TC]]
        c.Cbc2 = [_bc16[:, 2 * DS * TC:3 * DS * TC], _bc16[:, 3 * DS * TC:]]
        c.dA = [A.f32(128, TC) for _ in range(4)]
        c.beta = [A.f32(128, TC)] * 2
        c.h = [A.f32(128, TC)] * 2
        c.hC = [A.bf16(128, TC) for _ in range(2)]
        sel_full = A.f32(128, 16 * 128)
        c.sel16B = sel_full[0:16, :]
        c.sel16C = sel_full[32:48, :]
        c.setup_f = bcbc[:, 0:8192]
        c.setup_b = delta_reg.bitcast(BF16)[:, 0:8192]

        c.x_free_v = 0
        c.win_free_t = [0, 0]
        c.wout_free_t = [0, 0]
        c.convd_free_t = [0, 0]
        c.dA_free_v = [0, 0, 0, 0]
        c.hC_free_t = [0, 0]
        c.tsb_free_v = [0, 0]
        c.delta_done_v = [0, 0]
        c.scan_free_v = 0
        c.kt_free_v = [0] * DIT
        c.bc_free_v = [0, 0]
        c.ubf_free_v = 0
        c.xpn_free_dl = [0, 0]
        c.tap_dl = 0
        c.xdbl_done_s = [0, 0]
        c.bc_done_s = [0, 0]

        emit_setup(c)
        for u in prod_units(c, 0):
            u()
        for ch in range(NCH):
            nxt = prod_units(c, ch + 1) if ch + 1 < NCH else []
            emit_mamba_scan(c, ch, nxt)

        if debug == "mamba":
            c.wait("sync", c.dL, c.dL.n)
            c.wait("sync", c.dSc, 16 * NCH)
            c.wait("gpsimd", c.CC, NCH)
        else:
            A.reset()
            A.mark()
            c.ones_tf = A.f32(128, TF)
            c.xf_sb = A.bf16(128, DMT * TF)
            c.c_sb = A.f32(128, DMT * TF)
            c.cbf = A.bf16(128, DMT * TF)
            c.gate_w = {}
            for g in ("f", "i", "g"):
                c.gate_w[g] = (A.bf16(128, DMT * 1024), A.bf16(128, DMT * 1024))
            c.wo_sb = A.bf16(128, DMT * 1024)
            c.uo_sb = A.bf16(128, DMT * 1024)
            c.wv_sb = c.wo_sb
            c.woa_sb = c.uo_sb
            c.gbuf = {g: A.f32(128, TF) for g in ("f", "i", "g")}
            c.hbuf = A.f32(128, TF)
            c.obuf = [A.f32(128, TF) for _ in range(2)]
            c.res_buf = [A.f32(128, TF) for _ in range(2)]
            c.tmpf = A.f32(128, TF)
            c.tmpf2 = A.f32(128, TF)
            c.xtile = [A.f32(128, TF) for _ in range(2)]
            _sq = c.tmpf.bitcast(BF16)
            c.xsqt = [_sq[:, 0:TF], _sq[:, TF:2 * TF]]
            c.xnt = c.tmpf2
            c.cuo_bf = c.c_sb.bitcast(BF16)[:, 0:DMT * TF]
            c.vsumP = A.f32(128, 64)
            c.vsum_bf = A.bf16(128, 64)
            c.av_bf = A.bf16(128, 64)
            c.attnv = A.f32(128, 64)
            c.xtile_free_t = [0, 0]
            c.xtile_free_v = [0, 0]
            c.xtile_store_w = [None, None]
            c.xf_free_v = 0
            c.c0_free_v = 0
            c.res_free_v = [0, 0]
            c.hbuf_free_v = 0
            c.obuf_w = [None, None]
            c.xf_store_w = [None] * B
            c.x2_store_w = [[None] * DMT for _ in range(B)]
            c.x3_store_w = [[None] * DMT for _ in range(B)]
            c.wouo_w = []
            c.cbf_free_t = 0
            emit_phase2(c, out)
            c.wait("sync", c.dL, c.dL.n)
            c.pipe_drain("sync")

        block = e(nc.Block())
        for eng_name in ENGINES:
            getattr(block, eng_name)(
                lambda E, ops=tuple(c.ops[eng_name]): [f(E) for f in ops])
    return nc



def _pad_wxp(wxp):
    out = np.zeros((DI, 128), np.float32)
    out[:, 0:16] = wxp[:, 64:80]
    out[:, 32:48] = wxp[:, 80:96]
    out[:, 64:128] = wxp[:, 0:64]
    return out


def _stage(inputs):
    f32 = np.float32
    x = np.asarray(inputs["L0_attn_a_x"], f32)
    convw = np.asarray(inputs["L0_attn_a_conv_w"], f32)
    wqkv = np.asarray(inputs["L0_attn_b_w_qkv"], f32)
    alog = np.asarray(inputs["L0_attn_a_A_log"], f32)

    convd = np.zeros((128, 64 * 128), f32)
    for kt in range(DIT):
        for tap in range(4):
            blk = (kt * 4 + tap) * 128
            convd[np.arange(128), blk + np.arange(128)] = convw[kt * 128:(kt + 1) * 128, tap]

    sel16 = np.zeros((16, 16 * 128), f32)
    for n in range(16):
        sel16[n, n * 128:(n + 1) * 128] = 1.0

    def cols(v, nt):
        return np.ascontiguousarray(np.asarray(v, f32).reshape(nt, 128).T)

    def ct(v):
        return np.ascontiguousarray(np.asarray(v, f32).T)

    common = {
        "w_in": np.ascontiguousarray(np.asarray(inputs["L0_attn_a_W_in"], f32)),
        "w_xp": _pad_wxp(np.asarray(inputs["L0_attn_a_W_xproj"], f32)),
        "w_dt": np.ascontiguousarray(np.asarray(inputs["L0_attn_a_W_dt"], f32)),
        "w_out": np.ascontiguousarray(np.asarray(inputs["L0_attn_a_W_out"], f32)),
        "convd": convd,
        "alog": np.ascontiguousarray(
            alog.reshape(DIT, 128, DS).transpose(1, 0, 2).reshape(128, 256)),
        "gamma8": cols(inputs["L0_attn_a_ln_gamma"], DMT),
        "beta8": cols(inputs["L0_attn_a_ln_beta"], DMT),
        "convb16": cols(inputs["L0_attn_a_conv_b"], DIT),
        "bdt16": cols(inputs["L0_attn_a_b_dt"], DIT),
        "dsk16": cols(inputs["L0_attn_a_D_skip"], DIT),
        "bf8": cols(inputs["L0_ffn_bf"], DMT), "bi8": cols(inputs["L0_ffn_bi"], DMT),
        "bc8": cols(inputs["L0_ffn_bc"], DMT), "bo8": cols(inputs["L0_ffn_bo"], DMT),
        "sel16": sel16, "ident": np.eye(128, dtype=f32),
        "c0": np.ascontiguousarray(np.broadcast_to(
            np.asarray(inputs["L0_ffn_in_c"], f32)[:, None], (DM, TF))),
        "wfT": ct(inputs["L0_ffn_wf"]), "ufT": ct(inputs["L0_ffn_uf"]),
        "wiT": ct(inputs["L0_ffn_wi"]), "uiT": ct(inputs["L0_ffn_ui"]),
        "wcT": ct(inputs["L0_ffn_wc"]), "ucT": ct(inputs["L0_ffn_uc"]),
        "woT": ct(inputs["L0_ffn_wo"]), "uoT": ct(inputs["L0_ffn_uo"]),
        "wvT": np.ascontiguousarray(wqkv[2 * DM:3 * DM, :].T),
        "woaT": ct(inputs["L0_attn_b_w_o"]),
    }
    in_maps = []
    for i in range(NCORES):
        m = dict(common)
        m["xT"] = np.ascontiguousarray(x[i].T)
        gidx = np.zeros((128, 128), np.int32)
        for b in range(B):
            for kt in range(DMT):
                for h2 in range(2):
                    cc = 2 * i + h2
                    gidx[:, (b * DMT + kt) * 2 + h2] = (
                        (cc * NCORES + b) * DM + kt * 128 + np.arange(128))
        m["gidx"] = gidx
        in_maps.append(m)
    return in_maps


def kernel(**inputs):
    in_maps = _stage(inputs)
    nc = build()
    res = run_bass_kernel_spmd(nc, in_maps, core_ids=list(range(NCORES)))
    full = np.empty((B, L, DM), np.float32)
    for i in range(NCORES):
        o = res.results[i]["out"]
        full[:, TF * i:TF * (i + 1), :] = o.transpose(0, 2, 1)
    return full


# version 17 (final)
# speedup vs baseline: 1.0030x; 1.0030x over previous
import numpy as np
from contextlib import ExitStack

import concourse.bass as bass
import concourse.mybir as mybir
from concourse.bass import IndirectOffsetOnAxis
from concourse.bass_utils import run_bass_kernel_spmd

F32 = mybir.dt.float32
BF16 = mybir.dt.bfloat16
I32 = mybir.dt.int32
OP = mybir.AluOpType
ACT = mybir.ActivationFunctionType
AX = mybir.AxisListType

NCORES = 8
B, L, DM, DI, DS, RK = 8, 4096, 1024, 2048, 16, 64
DMT, DIT = DM // 128, DI // 128
TC, NCH = 256, L // 256
TF = 512
ET = 2 * DI // 128
EPS = 1e-5

ENGINES = ("sync", "vector", "scalar", "tensor", "gpsimd")


class Sem:
    def __init__(self, h):
        self.h = h
        self.n = 0

    def inc(self, k=1):
        self.n += k
        return self.n


class Bank:

    def __init__(self, ps, ready, free_s, free_v):
        self.ps = ps
        self.ready = ready
        self.free_s = free_s
        self.free_v = free_v


class Arena:

    def __init__(self, ap):
        self.ap = ap
        self.cap = ap.shape[1]
        self.off = 0
        self.marks = []

    def _take(self, words, align=16):
        o = (self.off + align - 1) // align * align
        assert o + words <= self.cap, f"arena overflow {o}+{words}>{self.cap}"
        self.off = o + words
        return o

    def f32(self, p, words):
        o = self._take(words)
        return self.ap[0:p, o:o + words]

    def bf16(self, p, n):
        words = (n + 1) // 2
        o = self._take(words)
        return self.ap[0:p, o:o + words].bitcast(BF16)[:, 0:n]

    def i32(self, p, n):
        o = self._take(n)
        return self.ap[0:p, o:o + n].bitcast(I32)

    def mark(self):
        self.marks.append(self.off)

    def reset(self):
        self.off = self.marks.pop()


class C:
    def __init__(self, nc):
        self.nc = nc
        self.ops = {e: [] for e in ENGINES}

    def op(self, eng, fn, inc=None, k=1, dma=False):
        if inc is not None:
            inc.inc(k)
            self.ops[eng].append(
                lambda E, fn=fn, h=inc.h, k=k: fn(E).then_inc(h, k))
            if eng in ("vector", "scalar", "tensor") and not dma:
                self.ops[eng].append(
                    lambda E, h=inc.h, vv=inc.n: E.wait_ge(h, vv))
        else:
            self.ops[eng].append(fn)

    def sync_dma(self, fn):
        self.op("sync", fn, inc=self.dL, k=16)
        self.wait("sync", self.dL)

    def gp_dma(self, fn):
        self.op("gpsimd", fn, inc=self.dG, k=16)
        self.wait("gpsimd", self.dG)

    def pipe_dma(self, eng, fn):
        sems = self.pd[eng]
        sem = sems[self.pd_i[eng] % 4]
        self.pd_i[eng] += 1
        self.wait(eng, sem)
        self.op(eng, fn, inc=sem, k=16)
        cnt = sem.n

        def w(eng2, sem=sem, cnt=cnt):
            self.wait(eng2, sem, cnt)
        return w

    def pipe_drain(self, eng2):
        for eng in ("sync", "gpsimd"):
            for sem in self.pd[eng]:
                self.wait(eng2, sem)

    def wait_free(self, eng, bank):
        self.wait(eng, bank.free_s)
        self.wait(eng, bank.free_v)

    def wait(self, eng, sem, v=None):
        vv = sem.n if v is None else v
        if vv > 0:
            self.ops[eng].append(lambda E, h=sem.h, vv=vv: E.wait_ge(h, vv))



def emit_setup(c):
    for fn in (
        lambda E: E.memset(c.ones_col, 1.0),
        lambda E: E.memset(c.ones_col_bf, 1.0),
        lambda E: E.memset(c.ones_row, 1.0),
        lambda E: E.memset(c.carry, 0.0),
        lambda E: E.memset(c.uf, 0.0),
    ):
        c.op("vector", fn)

    small = [
        (c.A_sb, c.t_alog), (c.gam, c.t_gamma), (c.bet, c.t_beta),
        (c.convb, c.t_convb), (c.bdt, c.t_bdt), (c.Dsk, c.t_dsk),
        (c.bf_c, c.t_bf), (c.bi_c, c.t_bi), (c.bc_c, c.t_bc), (c.bo_c, c.t_bo),
        (c.sel16B, c.t_sel16), (c.sel16C, c.t_sel16), (c.gidx, c.t_gidx),
    ]
    for dst, src in small:
        c.sync_dma(lambda E, d=dst, s=src: E.dma_start(d, s.ap()))
    small_loaded = c.dL.n
    c.wait("scalar", c.dL, small_loaded)
    c.op("scalar", lambda E: E.activation(c.A_sb, c.A_sb, ACT.Exp), inc=c.sS)
    c.wait("vector", c.sS)
    c.op("vector", lambda E: E.tensor_scalar(c.A_sb, c.A_sb, -1.0, None, OP.mult))
    c.wait("vector", c.dL, small_loaded)
    c.op("vector", lambda E: E.tensor_scalar(c.ngam, c.gam, -1.0, None, OP.mult),
         inc=c.sV)

    sc_f, sc_b = c.setup_f, c.setup_b

    def cast_stream(src_dram, n_rows, n_cols, dst_dram=None, dst_sb=None):
        for r2 in range(n_rows // 128):
            c.wait("sync", c.sV)
            c.sync_dma(lambda E, s=src_dram, r2=r2, w=n_cols: E.dma_start(
                sc_f[:, 0:w], bass.AP(s, r2 * 128 * w, [[w, 128], [1, w]])))
            c.wait("vector", c.dL)
            c.op("vector", lambda E, w=n_cols: E.tensor_copy(
                sc_b[:, 0:w], sc_f[:, 0:w]), inc=c.sV)
            c.wait("sync", c.sV)
            if dst_dram is not None:
                c.sync_dma(lambda E, d=dst_dram, r2=r2, w=n_cols: E.dma_start(
                    bass.AP(d, r2 * 128 * w, [[w, 128], [1, w]]), sc_b[:, 0:w]))
            else:
                c.sync_dma(lambda E, d=dst_sb, r2=r2, w=n_cols: E.dma_start(
                    d[:, r2 * w:(r2 + 1) * w], sc_b[:, 0:w]))

    cast_stream(c.t_win, DM, 4096, dst_dram=c.d_winbf)
    cast_stream(c.t_wout, DI, 1024, dst_dram=c.d_woutbf)
    cast_stream(c.t_wxp, DI, 128, dst_sb=c.wxp_sb)
    c.wait("sync", c.sV)
    c.sync_dma(lambda E: E.dma_start(sc_f[64:128, 0:DI], c.t_wdt.ap()))
    c.wait("vector", c.dL)
    c.op("vector", lambda E: E.tensor_copy(c.wdt_sb, sc_f[64:128, 0:DI]), inc=c.sV)
    c.wait("sync", c.sV)
    c.sync_dma(lambda E: E.dma_start(sc_f[:, 0:128], c.t_ident.ap()))
    c.wait("vector", c.dL)
    c.op("vector", lambda E: E.tensor_copy(c.ident_bf, sc_f[:, 0:128]), inc=c.sV)
    c.wait("sync", c.sV)
    for t_src, d_dst in c.ffn_casts:
        cast_stream(t_src, DM, 1024, dst_dram=d_dst)



def _emit_prod_ln(c, ch, par):
    if c.strong_barrier and ch == 0:
        for eng in ("sync", "vector", "scalar", "tensor", "gpsimd"):
            c.wait(eng, c.sV, c.sV.n)
            c.wait(eng, c.sS, c.sS.n)
            c.wait(eng, c.sT, c.sT.n)
            c.wait(eng, c.dL, c.dL.n)

    def vop(fn, inc=None):
        c.op("vector", fn, inc=inc)

    c.wait("sync", c.sV, c.x_free_v)
    c.op("sync", lambda E, ch=ch: E.dma_start(
        c.x_sb.rearrange("p (k t) -> p k t", k=DMT),
        bass.AP(c.t_x, ch * TC, [[L, 128], [128 * L, DMT], [1, TC]])),
        inc=c.dX, k=16)
    x_loaded = c.dX.n

    c.wait("scalar", c.dX, x_loaded)
    c.op("scalar", lambda E: E.activation(c.xsq_bf, c.x_sb, ACT.Square), inc=c.sS)
    xsq_done = c.sS.n
    bS, bS2 = c.banks[0], c.banks[1]
    c.wait_free("tensor", bS)
    c.wait("tensor", c.dX, x_loaded)
    for kt in range(DMT):
        c.op("tensor", lambda E, kt=kt: E.matmul(
            bS.ps[0:1, 0:TC], c.ones_col, c.x_sb[:, kt * TC:(kt + 1) * TC],
            start=(kt == 0), stop=(kt == DMT - 1)),
            inc=(bS.ready if kt == DMT - 1 else None))
    c.wait_free("tensor", bS2)
    c.wait("tensor", c.sS, xsq_done)
    for kt in range(DMT):
        c.op("tensor", lambda E, kt=kt: E.matmul(
            bS2.ps[0:1, 0:TC], c.ones_col_bf, c.xsq_bf[:, kt * TC:(kt + 1) * TC],
            start=(kt == 0), stop=(kt == DMT - 1)),
            inc=(bS2.ready if kt == DMT - 1 else None))

    r = c.rows
    c.wait("vector", bS.ready)
    c.wait("vector", bS2.ready)
    vop(lambda E: E.tensor_scalar(r["mu"], bS.ps[0:1, 0:TC], 1.0 / DM, None,
                                  OP.mult), inc=bS.free_v)
    vop(lambda E: E.tensor_scalar(r["ex2"], bS2.ps[0:1, 0:TC], 1.0 / DM, None,
                                  OP.mult), inc=bS2.free_v)
    vop(lambda E: E.tensor_tensor(r["mu2"], r["mu"], r["mu"], OP.mult))
    vop(lambda E: E.tensor_tensor(r["var"], r["ex2"], r["mu2"], OP.subtract))
    vop(lambda E: E.tensor_scalar(r["veps"], r["var"], EPS, None, OP.add))
    vop(lambda E: E.reciprocal(r["inv"], r["veps"]))
    vop(lambda E: E.tensor_scalar(r["vfrac"], r["inv"], -EPS, 1.0, OP.mult, OP.add))
    vop(lambda E: E.tensor_scalar(r["veps2"], r["vfrac"], EPS, None, OP.add))
    vop(lambda E: E.reciprocal(r["inv2"], r["veps2"]), inc=c.sV)
    inv_done = c.sV.n
    c.wait("scalar", c.sV, inv_done)
    c.op("scalar", lambda E: E.activation(r["r1"], r["inv"], ACT.Sqrt))
    c.op("scalar", lambda E: E.activation(r["r2"], r["inv2"], ACT.Sqrt), inc=c.sS)
    s_r2 = c.sS.n
    c.wait("vector", c.sS, s_r2)
    vop(lambda E: E.tensor_tensor(r["R"], r["r1"], r["r2"], OP.mult))
    vop(lambda E: E.tensor_tensor(r["muR"], r["mu"], r["R"], OP.mult))
    vop(lambda E: E.tensor_scalar(r["onepr"], r["r1"], 1.0, None, OP.add))
    vop(lambda E: E.tensor_tensor(r["mur"], r["mu"], r["r1"], OP.mult), inc=c.sV)
    rows_done = c.sV.n

    bA, bB = c.banks[2], c.banks[3]

    def bcast(bank, row):
        c.wait_free("tensor", bank)
        c.wait("tensor", c.sV, rows_done)
        c.op("tensor", lambda E, bank=bank, row=row: E.matmul(
            bank.ps[:, 0:TC], c.ones_row, row, start=True, stop=True),
            inc=bank.ready)

    xpn = c.xpn2[par]
    bcast(bA, r["onepr"])
    bcast(bB, r["mur"])
    c.wait("vector", bA.ready)
    c.wait("vector", bB.ready)
    c.wait("vector", c.dL, c.xpn_free_dl[par])
    for kt in range(DMT):
        sl = slice(kt * TC, (kt + 1) * TC)
        vop(lambda E, sl=sl, xpn=xpn: E.scalar_tensor_tensor(
            xpn[:, sl], c.x_sb[:, sl], 1.0, bA.ps[:, 0:TC], OP.mult, OP.mult))
        vop(lambda E, sl=sl, xpn=xpn: E.tensor_tensor(
            xpn[:, sl], xpn[:, sl], bB.ps[:, 0:TC], OP.subtract),
            inc=(bA.free_v if kt == DMT - 1 else None))
    vop(lambda E: E.nop(), inc=bB.free_v)

    bcast(bA, r["R"])
    bcast(bB, r["muR"])
    c.wait("vector", bA.ready)
    c.wait("vector", bB.ready)
    for kt in range(DMT):
        sl = slice(kt * TC, (kt + 1) * TC)
        vop(lambda E, sl=sl, kt=kt: E.scalar_tensor_tensor(
            c.tmp1, c.x_sb[:, sl], c.gam[:, kt:kt + 1], bA.ps[:, 0:TC],
            OP.mult, OP.mult))
        vop(lambda E, sl=sl, kt=kt: E.scalar_tensor_tensor(
            c.tmp2, bB.ps[:, 0:TC], c.ngam[:, kt:kt + 1], c.tmp1,
            OP.mult, OP.add))
        vop(lambda E, sl=sl, kt=kt: E.tensor_scalar(
            c.xn2bf[:, sl], c.tmp2, c.bet[:, kt:kt + 1], None, OP.add),
            inc=(bA.free_v if kt == DMT - 1 else None))
    vop(lambda E: E.nop(), inc=bB.free_v)
    vop(lambda E: E.nop(), inc=c.sV)
    c.xn2_done_v = c.sV.n
    c.x_free_v = c.xn2_done_v


def _prod_u_xz(c, ch, par, j):
    upf = c.upf2[par]
    siluz = c.siluz2[par]
    wpar = j % 2
    wb = c.win_buf[wpar]
    dw = c.dW0 if wpar == 0 else c.dW1
    c.wait("sync", c.sT, c.win_free_t[wpar])
    c.op("sync", lambda E, j=j, wb=wb: E.dma_start(
        wb.rearrange("p (k q) -> p k q", k=DMT),
        bass.AP(c.d_winbf, j * 128, [[4096, 128], [128 * 4096, DMT], [1, 128]])),
        inc=dw, k=16)
    wslab = dw.n
    bank = c.banks[4 + (j % 2)]
    c.wait_free("tensor", bank)
    c.wait("tensor", dw, wslab)
    c.wait("tensor", c.sV, c.xn2_done_v)
    for kt in range(DMT):
        c.op("tensor", lambda E, kt=kt, wb=wb, bank=bank: E.matmul(
            bank.ps[:, 0:TC], wb[:, kt * 128:(kt + 1) * 128],
            c.xn2bf[:, kt * TC:(kt + 1) * TC],
            start=(kt == 0), stop=(kt == DMT - 1)),
            inc=(bank.ready if kt == DMT - 1 else None))
    c.op("tensor", lambda E: E.nop(), inc=c.sT)
    c.win_free_t[wpar] = c.sT.n
    c.wait("scalar", bank.ready)
    if j < DIT:
        base = j * (TC + 4)
        c.op("scalar", lambda E, base=base, bank=bank: E.copy(
            c.uf[:, base + 4: base + 4 + TC], bank.ps[:, 0:TC]),
            inc=bank.free_s)
    else:
        k = j - DIT
        c.wait("scalar", c.sV, c.kt_free_v[k])
        c.op("scalar", lambda E, k=k, bank=bank, siluz=siluz: E.activation(
            siluz[:, k * TC:(k + 1) * TC], bank.ps[:, 0:TC], ACT.Silu),
            inc=bank.free_s)
    c.op("scalar", lambda E: E.nop(), inc=c.sS)
    c.xz_done_s = c.sS.n


def _prod_u_conv(c, ch, par, k):
    upf = c.upf2[par]
    cpar = k % 2
    cb = c.convd_buf[cpar]
    dc = c.dC0 if cpar == 0 else c.dC1
    c.wait("sync", c.sT, c.convd_free_t[cpar])
    c.op("sync", lambda E, k=k, cb=cb: E.dma_start(
        cb.rearrange("p (a q) -> p a q", a=4),
        bass.AP(c.t_convd, k * 512, [[64 * 128, 128], [128, 4], [1, 128]])),
        inc=dc, k=16)
    cslab = dc.n
    bank = c.banks[4 + (k % 2)]
    base = k * (TC + 4)
    c.wait_free("tensor", bank)
    c.wait("tensor", c.sS, c.xz_done_s)
    c.wait("tensor", dc, cslab)
    for tap in range(4):
        c.op("tensor", lambda E, tap=tap, cb=cb, bank=bank, base=base: E.matmul(
            bank.ps[:, 0:TC], cb[:, tap * 128:(tap + 1) * 128],
            c.uf[:, base + 1 + tap: base + 1 + tap + TC],
            start=(tap == 0), stop=(tap == 3)),
            inc=(bank.ready if tap == 3 else None))
    c.op("tensor", lambda E: E.nop(), inc=c.sT)
    c.convd_free_t[cpar] = c.sT.n
    c.wait("scalar", bank.ready)
    c.wait("scalar", c.sV, c.kt_free_v[k])
    c.op("scalar", lambda E, k=k, bank=bank, upf=upf: E.activation(
        upf[:, k * TC:(k + 1) * TC], bank.ps[:, 0:TC], ACT.Silu,
        bias=c.convb[:, k:k + 1]), inc=bank.free_s)
    c.op("scalar", lambda E, base=base: E.copy(
        c.uf[:, base + 1:base + 4], c.uf[:, base + 1 + TC:base + 4 + TC]),
        inc=c.sS)
    c.up_done_s = c.sS.n
    c.ubf_free_s = c.sS.n


def _prod_u_xdbl(c, ch, par):
    upf = c.upf2[par]
    bx = c.banks[4]
    c.wait_free("tensor", bx)
    c.wait("tensor", c.sS, c.up_done_s)
    for kt in range(DIT):
        c.op("tensor", lambda E, kt=kt, upf=upf: E.matmul(
            bx.ps[:, 0:TC], c.wxp_sb[:, kt * 128:(kt + 1) * 128],
            upf[:, kt * TC:(kt + 1) * TC],
            start=(kt == 0), stop=(kt == DIT - 1)),
            inc=(bx.ready if kt == DIT - 1 else None))
    c.wait("scalar", bx.ready)
    c.op("scalar", lambda E, par=par: E.copy(c.dt_bf2[par], bx.ps[64:128, 0:TC]))
    c.op("scalar", lambda E, par=par: E.copy(c.Bf2[par], bx.ps[0:16, 0:TC]))
    c.op("scalar", lambda E, par=par: E.copy(c.Cf2[par], bx.ps[32:48, 0:TC]),
         inc=bx.free_s)
    c.op("scalar", lambda E: E.nop(), inc=c.sS)
    c.xdbl_done_s[par] = c.sS.n


def _prod_u_bc(c, ch, par, n):
    Bbc, Cbc = c.Bbc2[par], c.Cbc2[par]
    xdbl_done = c.xdbl_done_s[par]
    for which, (sel, srcrow, dst) in enumerate(
            ((c.sel16B, c.Bf2[par], Bbc), (c.sel16C, c.Cf2[par], Cbc))):
        bank = c.banks[(2 * n + which) % 4]
        c.wait_free("tensor", bank)
        c.wait("tensor", c.sS, xdbl_done)
        c.op("tensor", lambda E, n=n, bank=bank, sel=sel, srcrow=srcrow: E.matmul(
            bank.ps[:, 0:TC], sel[:, n * 128:(n + 1) * 128], srcrow,
            start=True, stop=True), inc=bank.ready)
        c.wait("scalar", bank.ready)
        c.wait("scalar", c.sV, c.bc_free_v[par])
        c.op("scalar", lambda E, n=n, bank=bank, dst=dst: E.copy(
            dst[:, n * TC:(n + 1) * TC], bank.ps[:, 0:TC]), inc=bank.free_s)
    c.op("scalar", lambda E: E.nop(), inc=c.sS)
    c.bc_done_s[par] = c.sS.n


def _prod_u_delta(c, ch, par, kt):
    delta = c.delta2[par]
    dsl = slice(kt * TC, (kt + 1) * TC)
    tp2 = kt % 2
    bank = c.banks[4 + (kt % 2)]
    c.wait_free("tensor", bank)
    c.wait("tensor", c.sS, c.xdbl_done_s[par])
    c.op("tensor", lambda E, kt=kt, par=par, bank=bank: E.matmul(
        bank.ps[:, 0:TC], c.wdt_sb[:, kt * 128:(kt + 1) * 128], c.dt_bf2[par],
        start=True, stop=True), inc=bank.ready)
    c.wait("scalar", bank.ready)
    c.wait("scalar", c.sV, c.tsb_free_v[tp2])
    c.op("scalar", lambda E, kt=kt, bank=bank, tp2=tp2: E.activation(
        c.tsb1[tp2], bank.ps[:, 0:TC], ACT.Abs, bias=c.bdt[:, kt:kt + 1]))
    c.op("scalar", lambda E, kt=kt, bank=bank, tp2=tp2: E.activation(
        c.tsb2[tp2], bank.ps[:, 0:TC], ACT.Relu, bias=c.bdt[:, kt:kt + 1]),
        inc=bank.free_s)
    c.op("scalar", lambda E, tp2=tp2: E.activation(
        c.tsb1[tp2], c.tsb1[tp2], ACT.Exp, scale=-1.0))
    c.op("scalar", lambda E, tp2=tp2: E.activation(
        c.tsb1[tp2], c.tsb1[tp2], ACT.Ln, bias=1.0), inc=c.sS)
    sp_done = c.sS.n
    c.wait("vector", c.sS, sp_done)
    c.op("vector", lambda E, dsl=dsl, tp2=tp2, delta=delta: E.tensor_add(
        delta[:, dsl], c.tsb1[tp2], c.tsb2[tp2]), inc=c.sV)
    c.tsb_free_v[tp2] = c.sV.n
    c.delta_done_v[par] = c.sV.n


def prod_units(c, ch):
    par = ch % 2
    units = [lambda: _emit_prod_ln(c, ch, par)]
    for j in range(ET):
        units.append(lambda j=j: _prod_u_xz(c, ch, par, j))
    for k in range(DIT):
        units.append(lambda k=k: _prod_u_conv(c, ch, par, k))
    units.append(lambda: _prod_u_xdbl(c, ch, par))
    for n in range(DS):
        units.append(lambda n=n: _prod_u_bc(c, ch, par, n))
    for kt in range(DIT):
        units.append(lambda kt=kt: _prod_u_delta(c, ch, par, kt))
    return units


def emit_mamba_scan(c, ch, units=None):
    par = ch % 2
    units = units or []

    def vop(fn, inc=None):
        c.op("vector", fn, inc=inc)

    upf = c.upf2[par]
    siluz = c.siluz2[par]
    Bbc, Cbc = c.Bbc2[par], c.Cbc2[par]
    xpn = c.xpn2[par]
    delta = c.delta2[par]

    per_kt = (len(units) + DIT - 1) // DIT if units else 0

    for kt in range(DIT):
        dsl = slice(kt * TC, (kt + 1) * TC)
        vop(lambda E, dsl=dsl, upf=upf, delta=delta: E.tensor_tensor(
            c.du, delta[:, dsl], upf[:, dsl], OP.mult), inc=c.sV)
        du_done = c.sV.n
        c.wait("scalar", c.sV, du_done)
        c.wait("scalar", c.sV, c.delta_done_v[par])
        c.wait("vector", c.sS, c.bc_done_s[par])

        yb = c.banks[6 + (kt % 2)]
        for n in range(DS):
            pp = n % 4
            hp = n % 2
            col = kt * DS + n
            c.wait("scalar", c.sV, c.dA_free_v[pp])
            c.op("scalar", lambda E, dsl=dsl, col=col, pp=pp, delta=delta: E.activation(
                c.dA[pp], delta[:, dsl], ACT.Exp,
                scale=c.A_sb[:, col:col + 1]), inc=c.sS)
            dA_ready = c.sS.n
            c.wait("vector", c.sS, dA_ready)
            vop(lambda E, n=n, pp=pp, Bbc=Bbc: E.tensor_tensor(
                c.beta[pp % 2], c.du, Bbc[:, n * TC:(n + 1) * TC], OP.mult))
            vop(lambda E, col=col, pp=pp: E.tensor_tensor_scan(
                c.h[pp % 2], c.dA[pp], c.beta[pp % 2],
                c.carry[:, col:col + 1], OP.mult, OP.add), inc=c.sV)
            c.dA_free_v[pp] = c.sV.n
            vop(lambda E, col=col, pp=pp: E.tensor_copy(
                c.carry[:, col:col + 1], c.h[pp % 2][:, TC - 1:TC]))
            c.wait("vector", c.sT, c.hC_free_t[hp])
            vop(lambda E, n=n, pp=pp, hp=hp, Cbc=Cbc: E.tensor_tensor(
                c.hC[hp], c.h[pp % 2], Cbc[:, n * TC:(n + 1) * TC], OP.mult),
                inc=c.sV)
            hC_ready = c.sV.n
            if n == 0:
                c.wait_free("tensor", yb)
            c.wait("tensor", c.sV, hC_ready)
            c.op("tensor", lambda E, hp=hp, yb=yb, n=n: E.matmul(
                yb.ps[:, 0:TC], c.ident_bf, c.hC[hp],
                start=(n == 0), stop=(n == DS - 1)),
                inc=(yb.ready if n == DS - 1 else None))
            c.op("tensor", lambda E: E.nop(), inc=c.sT)
            c.hC_free_t[hp] = c.sT.n

        c.wait("vector", yb.ready)
        vop(lambda E, dsl=dsl, kt=kt, yb=yb, upf=upf: E.scalar_tensor_tensor(
            c.tmp1, upf[:, dsl], c.Dsk[:, kt:kt + 1], yb.ps[:, 0:TC],
            OP.mult, OP.add), inc=yb.free_v)
        vop(lambda E, dsl=dsl, siluz=siluz: E.tensor_tensor(
            c.yg[:, dsl], c.tmp1, siluz[:, dsl], OP.mult), inc=c.sV)
        c.kt_free_v[kt] = c.sV.n
        if units:
            for u in units[kt * per_kt:(kt + 1) * per_kt]:
                u()
    if units:
        for u in units[DIT * per_kt:]:
            u()
    yg_done = c.sV.n
    c.bc_free_v[par] = yg_done

    for j in range(DMT):
        wpar = j % 2
        wb = c.wout_buf[wpar]
        du_sem = c.dU0 if wpar == 0 else c.dU1
        c.wait("sync", c.sT, c.wout_free_t[wpar])
        c.op("sync", lambda E, j=j, wb=wb: E.dma_start(
            wb.rearrange("p (k q) -> p k q", k=DIT),
            bass.AP(c.d_woutbf, j * 128, [[1024, 128], [128 * 1024, DIT], [1, 128]])),
            inc=du_sem, k=16)
        wout_slab = du_sem.n
        bank = c.banks[4 + (j % 2)]
        c.wait_free("tensor", bank)
        c.wait("tensor", c.sV, yg_done)
        c.wait("tensor", du_sem, wout_slab)
        for kt in range(DIT):
            c.op("tensor", lambda E, kt=kt, wb=wb, bank=bank: E.matmul(
                bank.ps[:, 0:TC], wb[:, kt * 128:(kt + 1) * 128],
                c.yg[:, kt * TC:(kt + 1) * TC],
                start=(kt == 0), stop=(kt == DIT - 1)),
                inc=(bank.ready if kt == DIT - 1 else None))
        c.op("tensor", lambda E: E.nop(), inc=c.sT)
        c.wout_free_t[wpar] = c.sT.n
        c.wait("vector", bank.ready)
        vop(lambda E, j=j, bank=bank, xpn=xpn: E.tensor_tensor(
            xpn[:, j * TC:(j + 1) * TC], xpn[:, j * TC:(j + 1) * TC],
            bank.ps[:, 0:TC], OP.add), inc=bank.free_v)
    vop(lambda E: E.nop(), inc=c.sV)
    x1_done = c.sV.n

    c.wait("sync", c.sV, x1_done)
    c.sync_dma(lambda E, ch=ch, xpn=xpn: E.dma_start(
        bass.AP(c.d_agin[ch], 0, [[TC, 128], [128 * TC, DMT], [1, TC]]),
        xpn.rearrange("p (k t) -> p k t", k=DMT)))
    staged = c.dL.n
    c.xpn_free_dl[par] = staged
    if c.debug_x1 is not None:
        c.sync_dma(lambda E, ch=ch, xpn=xpn: E.dma_start(
            bass.AP(c.debug_x1, ch * TC, [[L, 128], [128 * L, DMT], [1, TC]]),
            xpn.rearrange("p (k t) -> p k t", k=DMT)))
        c.xpn_free_dl[par] = c.dL.n
    c.wait("gpsimd", c.dL, staged)
    c.op("gpsimd", lambda E, ch=ch: E.collective_compute(
        "AllGather", OP.bypass, [list(range(NCORES))],
        [c.d_agin[ch].ap()], [c.d_agout[ch].ap()]), inc=c.CC)
    c.wait("scalar", c.CC, c.CC.n)
    nelem = NCORES * DM * TC
    c.op("scalar", lambda E, ch=ch, nelem=nelem: E.dma_start(
        bass.AP(c.d_bigag, ch * nelem, [[65536, nelem // 65536], [1, 65536]]),
        bass.AP(c.d_agout[ch], 0, [[65536, nelem // 65536], [1, 65536]])),
        inc=c.dSc, k=16, dma=True)



def _xtile_wait(c, eng, par):
    c.wait(eng, c.sT, c.xtile_free_t[par])
    c.wait(eng, c.sV, c.xtile_free_v[par])
    if c.xtile_store_w[par] is not None:
        c.xtile_store_w[par](eng)


def emit_ln_stream(c, load_tile, apply_dst, nk, extra_per_tile=None):
    r = c.rows_f
    bS, bS2 = c.banks[0], c.banks[1]
    c.wait_free("tensor", bS)
    c.wait_free("tensor", bS2)
    for kt in range(nk):
        par = kt % 2
        w = load_tile(kt, par)
        w("scalar")
        c.wait("scalar", c.sT, c.xtile_free_t[par])
        c.op("scalar", lambda E, par=par: E.activation(
            c.xsqt[par], c.xtile[par], ACT.Square), inc=c.sS)
        sq = c.sS.n
        w("tensor")
        c.op("tensor", lambda E, par=par, kt=kt, nk=nk: E.matmul(
            bS.ps[0:1, 0:TF], c.ones_col, c.xtile[par],
            start=(kt == 0), stop=(kt == nk - 1)),
            inc=(bS.ready if kt == nk - 1 else None))
        c.wait("tensor", c.sS, sq)
        c.op("tensor", lambda E, par=par, kt=kt, nk=nk: E.matmul(
            bS2.ps[0:1, 0:TF], c.ones_col_bf, c.xsqt[par],
            start=(kt == 0), stop=(kt == nk - 1)),
            inc=(bS2.ready if kt == nk - 1 else None))
        c.op("tensor", lambda E: E.nop(), inc=c.sT)
        c.xtile_free_t[par] = c.sT.n
    c.wait("vector", bS.ready)
    c.wait("vector", bS2.ready)
    c.op("vector", lambda E: E.tensor_scalar(r["mu"], bS.ps[0:1, 0:TF],
                                             1.0 / DM, None, OP.mult), inc=bS.free_v)
    c.op("vector", lambda E: E.tensor_scalar(r["ex2"], bS2.ps[0:1, 0:TF],
                                             1.0 / DM, None, OP.mult), inc=bS2.free_v)
    c.op("vector", lambda E: E.tensor_tensor(r["mu2"], r["mu"], r["mu"], OP.mult))
    c.op("vector", lambda E: E.tensor_tensor(r["var"], r["ex2"], r["mu2"],
                                             OP.subtract))
    c.op("vector", lambda E: E.tensor_scalar(r["veps"], r["var"], EPS, None, OP.add))
    c.op("vector", lambda E: E.reciprocal(r["inv"], r["veps"]), inc=c.sV)
    c.wait("scalar", c.sV, c.sV.n)
    c.op("scalar", lambda E: E.activation(r["r1"], r["inv"], ACT.Sqrt), inc=c.sS)
    c.wait("vector", c.sS, c.sS.n)
    c.op("vector", lambda E: E.tensor_tensor(r["mur"], r["mu"], r["r1"], OP.mult),
         inc=c.sV)
    rows_done = c.sV.n
    bA, bB = c.banks[2], c.banks[3]
    for bank, row in ((bA, r["r1"]), (bB, r["mur"])):
        c.wait_free("tensor", bank)
        c.wait("tensor", c.sV, rows_done)
        c.op("tensor", lambda E, bank=bank, row=row: E.matmul(
            bank.ps[:, 0:TF], c.ones_row, row, start=True, stop=True),
            inc=bank.ready)
    c.wait("vector", bA.ready)
    c.wait("vector", bB.ready)
    for kt in range(nk):
        par = kt % 2
        w = load_tile(kt, par)
        w("vector")
        c.wait("vector", c.sT, c.xtile_free_t[par])
        c.op("vector", lambda E, par=par: E.scalar_tensor_tensor(
            c.xnt, c.xtile[par], 1.0, bA.ps[:, 0:TF], OP.mult, OP.mult))
        c.op("vector", lambda E, par=par, kt=kt: E.tensor_tensor(
            apply_dst(kt), c.xnt, bB.ps[:, 0:TF], OP.subtract), inc=c.sV)
        c.xtile_free_v[par] = c.sV.n
        if extra_per_tile is not None:
            extra_per_tile(kt)
    c.op("vector", lambda E: E.nop(), inc=bA.free_v)
    c.op("vector", lambda E: E.nop(), inc=bB.free_v)
    c.op("vector", lambda E: E.nop(), inc=c.sV)
    return c.sV.n


def mk_gather_loader(c, t):
    def load_tile(kt, par):
        jcol0 = (t * DMT + kt) * 2
        _xtile_wait(c, "gpsimd", par)
        ws = []
        for h2 in range(2):
            ws.append(c.pipe_dma("gpsimd", lambda E, par=par, jc=jcol0 + h2,
                                 h2=h2: E.indirect_dma_start(
                c.xtile[par][:, h2 * TC:(h2 + 1) * TC], None, c.d_bigag.ap(),
                IndirectOffsetOnAxis(c.gidx[:, jc:jc + 1], 0))))

        def w(eng, ws=ws):
            for wi in ws:
                wi(eng)
        return w
    return load_tile


def mk_dram_loader(c, dram, t, store_w=None):
    def load_tile(kt, par):
        _xtile_wait(c, "sync", par)
        if store_w is not None and store_w[t][kt] is not None:
            store_w[t][kt]("sync")
        return c.pipe_dma("sync", lambda E, kt=kt, par=par: E.dma_start(
            c.xtile[par],
            bass.AP(dram, t * DM * TF + kt * 128 * TF, [[TF, 128], [1, TF]])))
    return load_tile



def emit_ffn(c, resid_loader, out_writer):
    c.wait("sync", c.sV, c.c0_free_v)
    c.sync_dma(lambda E: E.dma_start(
        c.c_sb.rearrange("p (k t) -> p k t", k=DMT),
        bass.AP(c.d_c0, 0, [[TF, 128], [128 * TF, DMT], [1, TF]])))
    c0_loaded = c.dL.n
    c.wait("vector", c.dL, c0_loaded)

    for t in range(B):
        c.wait("vector", c.sT, c.cbf_free_t)
        c.op("vector", lambda E: E.tensor_copy(c.cbf, c.c_sb), inc=c.sV)
        cbf_done = c.sV.n
        c.wait("sync", c.sV, c.xf_free_v)
        if c.xf_store_w[t] is not None:
            c.xf_store_w[t]("sync")
        xfw = c.pipe_dma("sync", lambda E, t=t: E.dma_start(
            c.xf_sb.rearrange("p (k t2) -> p k t2", k=DMT),
            bass.AP(c.d_xf, t * DM * TF, [[TF, 128], [128 * TF, DMT], [1, TF]])))
        for j in range(DMT):
            for gi, g in enumerate(("f", "i", "g")):
                bank = c.banks[4 + gi]
                wsb, usb = c.gate_w[g]
                c.wait_free("tensor", bank)
                xfw("tensor")
                c.wait("tensor", c.sV, cbf_done)
                for kt in range(DMT):
                    c.op("tensor", lambda E, j=j, kt=kt, wsb=wsb, bank=bank: E.matmul(
                        bank.ps[:, 0:TF],
                        wsb[:, kt * 1024 + j * 128: kt * 1024 + (j + 1) * 128],
                        c.xf_sb[:, kt * TF:(kt + 1) * TF],
                        start=(kt == 0), stop=False))
                for kt in range(DMT):
                    c.op("tensor", lambda E, j=j, kt=kt, usb=usb, bank=bank: E.matmul(
                        bank.ps[:, 0:TF],
                        usb[:, kt * 1024 + j * 128: kt * 1024 + (j + 1) * 128],
                        c.cbf[:, kt * TF:(kt + 1) * TF],
                        start=False, stop=(kt == DMT - 1)),
                        inc=(bank.ready if kt == DMT - 1 else None))
                fn = ACT.Sigmoid if g in ("f", "i") else ACT.Tanh
                bias = {"f": c.bf_c, "i": c.bi_c, "g": c.bc_c}[g]
                c.wait("scalar", bank.ready)
                c.op("scalar", lambda E, g=g, j=j, bank=bank, fn=fn, bias=bias:
                     E.activation(c.gbuf[g], bank.ps[:, 0:TF], fn,
                                  bias=bias[:, j:j + 1]), inc=c.sS)
            gates_done = c.sS.n
            csl = slice(j * TF, (j + 1) * TF)
            c.wait("vector", c.sS, gates_done)
            c.op("vector", lambda E: E.tensor_tensor(
                c.tmpf, c.gbuf["i"], c.gbuf["g"], OP.mult))
            c.op("vector", lambda E, csl=csl: E.tensor_tensor(
                c.tmpf2, c.gbuf["f"], c.c_sb[:, csl], OP.mult))
            for gi in range(3):
                c.op("vector", lambda E: E.nop(), inc=c.banks[4 + gi].free_v)
            c.op("vector", lambda E, csl=csl: E.tensor_tensor(
                c.c_sb[:, csl], c.tmpf, c.tmpf2, OP.add), inc=c.sV)
        c.op("vector", lambda E: E.nop(), inc=c.sV)
        c.xf_free_v = c.sV.n
        c.op("tensor", lambda E: E.nop(), inc=c.sT)
        c.cbf_free_t = c.sT.n

    c.wait("vector", c.sT, c.cbf_free_t)
    c.op("vector", lambda E: E.tensor_copy(c.cbf, c.c_sb), inc=c.sV)
    cbf_fin = c.sV.n
    for j in range(DMT):
        bank = c.banks[4 + (j % 2)]
        c.wait_free("tensor", bank)
        c.wait("tensor", c.sV, cbf_fin)
        for wz in c.wouo_w:
            wz("tensor")
        for kt in range(DMT):
            c.op("tensor", lambda E, j=j, kt=kt, bank=bank: E.matmul(
                bank.ps[:, 0:TF],
                c.uo_sb[:, kt * 1024 + j * 128: kt * 1024 + (j + 1) * 128],
                c.cbf[:, kt * TF:(kt + 1) * TF],
                start=(kt == 0), stop=(kt == DMT - 1)),
                inc=(bank.ready if kt == DMT - 1 else None))
        c.wait("scalar", bank.ready)
        c.op("scalar", lambda E, j=j, bank=bank: E.copy(
            c.cuo_bf[:, j * TF:(j + 1) * TF], bank.ps[:, 0:TF]), inc=bank.free_s)
    c.op("scalar", lambda E: E.nop(), inc=c.sS)
    cuo_done = c.sS.n

    for t in range(B):
        c.wait("sync", c.sV, c.xf_free_v)
        if c.xf_store_w[t] is not None:
            c.xf_store_w[t]("sync")
        xfw = c.pipe_dma("sync", lambda E, t=t: E.dma_start(
            c.xf_sb.rearrange("p (k t2) -> p k t2", k=DMT),
            bass.AP(c.d_xf, t * DM * TF, [[TF, 128], [128 * TF, DMT], [1, TF]])))
        for j in range(DMT):
            bank = c.banks[4 + (j % 2)]
            c.wait_free("tensor", bank)
            xfw("tensor")
            c.wait("tensor", c.sS, cuo_done)
            for kt in range(DMT):
                c.op("tensor", lambda E, j=j, kt=kt, bank=bank: E.matmul(
                    bank.ps[:, 0:TF],
                    c.wo_sb[:, kt * 1024 + j * 128: kt * 1024 + (j + 1) * 128],
                    c.xf_sb[:, kt * TF:(kt + 1) * TF],
                    start=(kt == 0), stop=False))
            c.op("tensor", lambda E, j=j, bank=bank: E.matmul(
                bank.ps[:, 0:TF], c.ident_bf, c.cuo_bf[:, j * TF:(j + 1) * TF],
                start=False, stop=True), inc=bank.ready)
            hi = j % 2
            c.wait("scalar", bank.ready)
            c.wait("scalar", c.sV, c.hbuf_free_v[hi])
            c.op("scalar", lambda E, j=j, bank=bank, hi=hi: E.activation(
                c.hbuf[hi], bank.ps[:, 0:TF], ACT.Tanh, bias=c.bo_c[:, j:j + 1]),
                inc=c.sS)
            h_done = c.sS.n
            pi = (t * DMT + j) % 2
            res_w = resid_loader(t, j)
            c.wait("vector", c.sS, h_done)
            res_w("vector")
            if c.obuf_w[pi] is not None:
                c.obuf_w[pi]("vector")
            c.op("vector", lambda E, pi=pi, hi=hi: E.tensor_tensor(
                c.obuf[pi], c.res_buf[pi], c.hbuf[hi], OP.add), inc=c.sV)
            c.hbuf_free_v[hi] = c.sV.n
            c.res_free_v[pi] = c.sV.n
            c.op("vector", lambda E, bank=bank: E.nop(), inc=bank.free_v)
            out_writer(t, j, c.sV.n)
        c.op("vector", lambda E: E.nop(), inc=c.sV)
        c.xf_free_v = c.sV.n
    c.op("tensor", lambda E: E.nop(), inc=c.sT)



def emit_phase2(c, out):
    mam = dict(sV=c.sV.n, sS=c.sS.n, sT=c.sT.n, dL=c.dL.n)
    for eng in ENGINES:
        c.wait(eng, c.CC, NCH)
        c.wait(eng, c.dSc, 16 * NCH)
        for nm in ("sV", "sS", "sT", "dL"):
            c.wait(eng, getattr(c, nm), mam[nm])
    c.op("vector", lambda E: E.memset(c.ones_tf, 1.0))

    def load_w(dst, dram):
        return c.pipe_dma("sync", lambda E, dst=dst, dram=dram: E.dma_start(
            dst.rearrange("p (k q) -> p k q", k=DMT),
            bass.AP(dram, 0, [[1024, 128], [128 * 1024, DMT], [1, 1024]])))

    for g, wn, un in (("f", "wfT", "ufT"), ("i", "wiT", "uiT"), ("g", "wcT", "ucT")):
        load_w(c.gate_w[g][0], c.ffn_bf[wn])
        load_w(c.gate_w[g][1], c.ffn_bf[un])
    c.wouo_w = [load_w(c.wo_sb, c.ffn_bf["woT"]),
                load_w(c.uo_sb, c.ffn_bf["uoT"])]
    c.pipe_drain("tensor")

    for t in range(B):
        if t > 0:
            c.xf_store_w[t - 1]("vector")
        done = emit_ln_stream(
            c, mk_gather_loader(c, t),
            lambda kt: c.xf_sb[:, kt * TF:(kt + 1) * TF], DMT)
        c.wait("sync", c.sV, done)
        c.xf_store_w[t] = c.pipe_dma("sync", lambda E, t=t: E.dma_start(
            bass.AP(c.d_xf, t * DM * TF, [[TF, 128], [128 * TF, DMT], [1, TF]]),
            c.xf_sb.rearrange("p (k t2) -> p k t2", k=DMT)))
        c.op("vector", lambda E: E.nop(), inc=c.sV)
    c.xf_free_v = c.sV.n

    def resid1(t, j):
        jcol0 = (t * DMT + j) * 2
        pi = (t * DMT + j) % 2
        c.wait("gpsimd", c.sV, c.res_free_v[pi])
        ws = []
        for h2 in range(2):
            ws.append(c.pipe_dma("gpsimd", lambda E, jc=jcol0 + h2, h2=h2,
                                 pi=pi: E.indirect_dma_start(
                c.res_buf[pi][:, h2 * TC:(h2 + 1) * TC], None, c.d_bigag.ap(),
                IndirectOffsetOnAxis(c.gidx[:, jc:jc + 1], 0))))

        def w(eng, ws=ws):
            for wi in ws:
                wi(eng)
        return w

    def write_x2(t, j, v_done):
        pi = (t * DMT + j) % 2
        c.wait("sync", c.sV, v_done)
        c.obuf_w[pi] = c.pipe_dma("sync", lambda E, t=t, j=j, pi=pi: E.dma_start(
            bass.AP(c.d_x2, t * DM * TF + j * 128 * TF, [[TF, 128], [1, TF]]),
            c.obuf[pi]))
        c.x2_store_w[t][j] = c.obuf_w[pi]

    emit_ffn(c, resid1, write_x2)

    for t in range(B):
        def extra(kt, t=t):
            c.op("vector", lambda E, kt=kt, t=t: E.tensor_reduce(
                c.vsumP[:, kt * B + t: kt * B + t + 1], c.xnt, AX.X, OP.add))
        done = emit_ln_stream(
            c, mk_dram_loader(c, c.d_x2, t, store_w=c.x2_store_w),
            lambda kt: c.xnt, DMT, extra_per_tile=extra)
    vsum_done = done
    c.wait("sync", c.sV, vsum_done)
    c.sync_dma(lambda E: E.dma_start(c.d_ar_in.ap(), c.vsumP))
    c.wait("gpsimd", c.dL, c.dL.n)
    c.op("gpsimd", lambda E: E.collective_compute(
        "AllReduce", OP.add, [list(range(NCORES))],
        [c.d_ar_in.ap()], [c.d_ar_out.ap()]), inc=c.CC)
    c.wait("sync", c.CC, c.CC.n)
    c.sync_dma(lambda E: E.dma_start(c.vsumP, c.d_ar_out.ap()))
    vs_back = c.dL.n
    c.wait("sync", c.sT, c.sT.n)
    wvw = [load_w(c.wv_sb, c.ffn_bf["wvT"]),
           load_w(c.woa_sb, c.ffn_bf["woaT"])]
    c.wait("vector", c.dL, vs_back)
    c.op("vector", lambda E: E.tensor_copy(c.vsum_bf, c.vsumP), inc=c.sV)
    vbf_done = c.sV.n
    bank = c.banks[6]
    c.wait("tensor", c.sV, vbf_done)
    for wz in wvw:
        wz("tensor")
    for j in range(DMT):
        c.wait_free("tensor", bank)
        for kt in range(DMT):
            c.op("tensor", lambda E, j=j, kt=kt: E.matmul(
                bank.ps[:, 0:B],
                c.wv_sb[:, kt * 1024 + j * 128: kt * 1024 + (j + 1) * 128],
                c.vsum_bf[:, kt * B:(kt + 1) * B],
                start=(kt == 0), stop=(kt == DMT - 1)),
                inc=(bank.ready if kt == DMT - 1 else None))
        c.wait("scalar", bank.ready)
        c.op("scalar", lambda E, j=j: E.copy(
            c.av_bf[:, j * B:(j + 1) * B], bank.ps[:, 0:B]), inc=bank.free_s)
    c.op("scalar", lambda E: E.nop(), inc=c.sS)
    av_done = c.sS.n
    c.wait("tensor", c.sS, av_done)
    for j in range(DMT):
        c.wait_free("tensor", bank)
        for kt in range(DMT):
            c.op("tensor", lambda E, j=j, kt=kt: E.matmul(
                bank.ps[:, 0:B],
                c.woa_sb[:, kt * 1024 + j * 128: kt * 1024 + (j + 1) * 128],
                c.av_bf[:, kt * B:(kt + 1) * B],
                start=(kt == 0), stop=(kt == DMT - 1)),
                inc=(bank.ready if kt == DMT - 1 else None))
        c.wait("scalar", bank.ready)
        c.op("scalar", lambda E, j=j: E.copy(
            c.attnv[:, j * B:(j + 1) * B], bank.ps[:, 0:B]), inc=bank.free_s)
    c.op("scalar", lambda E: E.nop(), inc=c.sS)
    attnv_done = c.sS.n
    c.op("tensor", lambda E: E.nop(), inc=c.sT)
    c.wait("sync", c.sT, c.sT.n)
    c.wouo_w = [load_w(c.wo_sb, c.ffn_bf["woT"]),
                load_w(c.uo_sb, c.ffn_bf["uoT"])]

    c.wait("vector", c.sS, attnv_done)
    c.wait("vector", c.dL, c.dL.n)
    for t in range(B):
        for kt in range(DMT):
            par = kt % 2
            _xtile_wait(c, "sync", par)
            if c.x2_store_w[t][kt] is not None:
                c.x2_store_w[t][kt]("sync")
            x2w = c.pipe_dma("sync", lambda E, t=t, kt=kt, par=par: E.dma_start(
                c.xtile[par],
                bass.AP(c.d_x2, t * DM * TF + kt * 128 * TF, [[TF, 128], [1, TF]])))
            x2w("vector")
            c.op("vector", lambda E, kt=kt, t=t, par=par: E.scalar_tensor_tensor(
                c.xtile[par], c.ones_tf, c.attnv[:, kt * B + t: kt * B + t + 1],
                c.xtile[par], OP.mult, OP.add), inc=c.sV)
            x3v = c.sV.n
            c.wait("sync", c.sV, x3v)
            sw = c.pipe_dma("sync", lambda E, t=t, kt=kt, par=par: E.dma_start(
                bass.AP(c.d_x3, t * DM * TF + kt * 128 * TF, [[TF, 128], [1, TF]]),
                c.xtile[par]))
            c.x3_store_w[t][kt] = sw
            c.xtile_store_w[par] = sw
            c.xtile_free_v[kt % 2] = c.sV.n
        if t > 0:
            c.xf_store_w[t - 1]("vector")
        done = emit_ln_stream(
            c, mk_dram_loader(c, c.d_x3, t, store_w=c.x3_store_w),
            lambda kt: c.xf_sb[:, kt * TF:(kt + 1) * TF], DMT)
        c.wait("sync", c.sV, done)
        c.xf_store_w[t] = c.pipe_dma("sync", lambda E, t=t: E.dma_start(
            bass.AP(c.d_xf, t * DM * TF, [[TF, 128], [128 * TF, DMT], [1, TF]]),
            c.xf_sb.rearrange("p (k t2) -> p k t2", k=DMT)))
        c.op("vector", lambda E: E.nop(), inc=c.sV)
        c.xf_free_v = c.sV.n

    c.c0_free_v = c.sV.n

    def resid2(t, j):
        pi = (t * DMT + j) % 2
        c.wait("sync", c.sV, c.res_free_v[pi])
        if c.x3_store_w[t][j] is not None:
            c.x3_store_w[t][j]("sync")
        return c.pipe_dma("sync", lambda E, t=t, j=j, pi=pi: E.dma_start(
            c.res_buf[pi],
            bass.AP(c.d_x3, t * DM * TF + j * 128 * TF, [[TF, 128], [1, TF]])))

    def write_out(t, j, v_done):
        pi = (t * DMT + j) % 2
        c.wait("sync", c.sV, v_done)
        c.obuf_w[pi] = c.pipe_dma("sync", lambda E, t=t, j=j, pi=pi: E.dma_start(
            bass.AP(out, t * DM * TF + j * 128 * TF, [[TF, 128], [1, TF]]),
            c.obuf[pi]))

    emit_ffn(c, resid2, write_out)



def build(debug=None):
    nc = bass.Bass("TRN2", target_bir_lowering=False)
    nc.detect_race_conditions = False
    c = C(nc)
    t = {}

    def inp(name, shape, dt=F32):
        t[name] = nc.dram_tensor(name, shape, dt, kind="ExternalInput")
        return t[name]

    c.t_x = inp("xT", [DM, L])
    c.t_win = inp("w_in", [DM, 2 * DI])
    c.t_wxp = inp("w_xp", [DI, 128])
    c.t_wdt = inp("w_dt", [RK, DI])
    c.t_wout = inp("w_out", [DI, DM])
    c.t_convd = inp("convd", [128, 64 * 128])
    c.t_alog = inp("alog", [128, 256])
    c.t_gamma = inp("gamma8", [128, DMT])
    c.t_beta = inp("beta8", [128, DMT])
    c.t_convb = inp("convb16", [128, DIT])
    c.t_bdt = inp("bdt16", [128, DIT])
    c.t_dsk = inp("dsk16", [128, DIT])
    c.t_bf = inp("bf8", [128, DMT])
    c.t_bi = inp("bi8", [128, DMT])
    c.t_bc = inp("bc8", [128, DMT])
    c.t_bo = inp("bo8", [128, DMT])
    c.t_sel16 = inp("sel16", [16, 16 * 128])
    c.t_ident = inp("ident", [128, 128])
    c.t_gidx = inp("gidx", [128, 128], I32)
    c.t_c0 = inp("c0", [DM, TF])
    ffn_names = ["wfT", "ufT", "wiT", "uiT", "wcT", "ucT", "woT", "uoT",
                 "wvT", "woaT"]
    for nm in ffn_names:
        inp(nm, [DM, DM])

    out = nc.dram_tensor("out", [B, DM, TF], F32, kind="ExternalOutput")
    c.debug_x1 = None
    c.dbg = {}
    c.strong_barrier = True
    if debug == "mamba":
        c.debug_x1 = nc.dram_tensor("dbg_x1", [DM, L], F32, kind="ExternalOutput")
        for nm, shp, dt2 in [("up", [DI, L], F32), ("sz", [DI, L], F32),
                             ("dl", [DI, L], F32), ("yg", [DI, L], BF16),
                             ("bc", [128, L], F32), ("xn2", [DM, L], BF16),
                             ("ubf", [128, NCH * DIT * (TC + 4)], F32)]:
            c.dbg[nm] = nc.dram_tensor("dbg_" + nm, shp, dt2, kind="ExternalOutput")

    c.d_winbf = nc.dram_tensor("winbf", [DM, 2 * DI], BF16, kind="Internal")
    c.d_woutbf = nc.dram_tensor("woutbf", [DI, DM], BF16, kind="Internal")
    c.ffn_bf = {nm: nc.dram_tensor(nm + "bf", [DM, DM], BF16, kind="Internal")
                for nm in ffn_names}
    c.ffn_casts = [(t[nm], c.ffn_bf[nm]) for nm in ffn_names]
    c.d_agin = [nc.dram_tensor(f"agin{i}", [DM, TC], F32, kind="Internal")
                for i in range(NCH)]
    c.d_agout = [nc.dram_tensor(f"agout{i}", [NCORES * DM, TC], F32, kind="Internal")
                 for i in range(NCH)]
    c.d_bigag = nc.dram_tensor("bigag", [NCH * NCORES * DM, TC], F32,
                               kind=("ExternalOutput" if debug == "full" else "Internal"))
    c.d_c0 = t["c0"]
    scratch_kind = "ExternalOutput" if debug == "full" else "Internal"
    c.d_xf = nc.dram_tensor("xfscr", [B * DM, TF], BF16, kind=scratch_kind)
    c.d_x2 = nc.dram_tensor("x2scr", [B * DM, TF], F32, kind=scratch_kind)
    c.d_x3 = nc.dram_tensor("x3scr", [B * DM, TF], F32, kind=scratch_kind)
    c.d_ar_in = nc.dram_tensor("arin", [128, 64], F32, kind="Internal")
    c.d_ar_out = nc.dram_tensor("arout", [128, 64], F32, kind="Internal")

    with ExitStack() as ctx:
        e = ctx.enter_context
        arena_t = e(nc.sbuf_tensor("arena", [128, 53000], F32))
        A = Arena(arena_t[:])
        for nm in ["dL", "dSc", "dG", "CC", "sV", "sS", "sT", "sG",
                   "dX", "dW0", "dW1", "dU0", "dU1", "dC0", "dC1"]:
            setattr(c, nm, Sem(e(nc.semaphore(nm))))
        c.pd = {"sync": [Sem(e(nc.semaphore(f"dP{i}"))) for i in range(4)],
                "gpsimd": [Sem(e(nc.semaphore(f"dQ{i}"))) for i in range(4)]}
        c.pd_i = {"sync": 0, "gpsimd": 0}
        c.banks = []
        for i in range(8):
            ps = e(nc.psum_tensor(f"P{i}", [128, 512], F32))
            c.banks.append(Bank(ps[:], Sem(e(nc.semaphore(f"bkr{i}"))),
                                Sem(e(nc.semaphore(f"bkfs{i}"))),
                                Sem(e(nc.semaphore(f"bkfv{i}")))))

        c.ones_col = A.f32(128, 1)
        c.ones_col_bf = A.bf16(128, 2)[:, 0:1]
        c.ones_row = A.f32(1, 128)
        c.A_sb = A.f32(128, 256)
        c.gam = A.f32(128, DMT)
        c.ngam = A.f32(128, DMT)
        c.bet = A.f32(128, DMT)
        c.convb = A.f32(128, DIT)
        c.bdt = A.f32(128, DIT)
        c.Dsk = A.f32(128, DIT)
        c.bf_c = A.f32(128, DMT)
        c.bi_c = A.f32(128, DMT)
        c.bc_c = A.f32(128, DMT)
        c.bo_c = A.f32(128, DMT)
        c.carry = A.f32(128, 256)
        c.ident_bf = A.bf16(128, 128)
        c.gidx = A.i32(128, 128)
        rows_all = A.f32(1, 16 * 256)
        c.rows = {}
        for i2, nm in enumerate(["mu", "ex2", "mu2", "var", "veps", "inv",
                                 "r1", "vfrac", "veps2", "inv2", "r2", "R",
                                 "muR", "onepr", "mur"]):
            c.rows[nm] = rows_all[:, i2 * 256:i2 * 256 + TC]
        c.rows_f = {}
        for i2, nm in enumerate(["mu", "ex2", "mu2", "var", "veps", "inv",
                                 "r1", "mur"]):
            c.rows_f[nm] = rows_all[:, i2 * TF:(i2 + 1) * TF]

        A.mark()
        c.wout_buf = [A.bf16(128, DIT * 128) for _ in range(2)]
        c.convd_buf = [A.f32(128, 512) for _ in range(2)]
        c.wxp_sb = A.bf16(128, DIT * 128)
        _wdt_reg = A.f32(128, DI // 2)
        c.wdt_sb = _wdt_reg.bitcast(BF16)[64:128, 0:DI]
        c.win_buf = [A.bf16(128, DMT * 128) for _ in range(2)]
        c.x_sb = A.f32(128, DMT * TC)
        c.xpn2 = [A.f32(128, DMT * TC) for _ in range(2)]
        c.xn2bf = A.bf16(128, DMT * TC)
        c.xsq_bf = A.bf16(128, DMT * TC)
        c.uf = A.f32(128, DIT * (TC + 4))
        c.upf2 = [A.bf16(128, DIT * TC) for _ in range(2)]
        c.siluz2 = [A.bf16(128, DIT * TC) for _ in range(2)]
        delta_reg = A.f32(128, DIT * TC)
        _d16 = delta_reg.bitcast(BF16)
        c.delta2 = [_d16[:, 0:DIT * TC], _d16[:, DIT * TC:]]
        c.du = A.f32(128, TC)
        c.yg = A.bf16(128, DIT * TC)
        c.tmp1 = A.f32(128, TC)
        c.tmp2 = A.f32(128, TC)
        c.tsb1 = [A.f32(128, TC) for _ in range(2)]
        c.tsb2 = [A.f32(128, TC) for _ in range(2)]
        _dt_reg = [A.f32(128, TC // 2) for _ in range(2)]
        c.dt_bf2 = [t.bitcast(BF16)[64:128, 0:TC] for t in _dt_reg]
        bc_strip = [A.f32(128, TC) for _ in range(2)]
        c.Bf2 = [t[0:16, :] for t in bc_strip]
        c.Cf2 = [t[32:48, :] for t in bc_strip]
        bcbc = A.f32(128, 2 * DS * TC)
        _bc16 = bcbc.bitcast(BF16)
        c.Bbc2 = [_bc16[:, 0:DS * TC], _bc16[:, DS * TC:2 * DS * TC]]
        c.Cbc2 = [_bc16[:, 2 * DS * TC:3 * DS * TC], _bc16[:, 3 * DS * TC:]]
        c.dA = [A.f32(128, TC) for _ in range(4)]
        c.beta = [A.f32(128, TC)] * 2
        c.h = [A.f32(128, TC)] * 2
        c.hC = [A.bf16(128, TC) for _ in range(2)]
        sel_full = A.f32(128, 16 * 128)
        c.sel16B = sel_full[0:16, :]
        c.sel16C = sel_full[32:48, :]
        c.setup_f = bcbc[:, 0:8192]
        c.setup_b = delta_reg.bitcast(BF16)[:, 0:8192]

        c.x_free_v = 0
        c.win_free_t = [0, 0]
        c.wout_free_t = [0, 0]
        c.convd_free_t = [0, 0]
        c.dA_free_v = [0, 0, 0, 0]
        c.hC_free_t = [0, 0]
        c.tsb_free_v = [0, 0]
        c.delta_done_v = [0, 0]
        c.scan_free_v = 0
        c.kt_free_v = [0] * DIT
        c.bc_free_v = [0, 0]
        c.ubf_free_v = 0
        c.xpn_free_dl = [0, 0]
        c.tap_dl = 0
        c.xdbl_done_s = [0, 0]
        c.bc_done_s = [0, 0]

        emit_setup(c)
        for u in prod_units(c, 0):
            u()
        for ch in range(NCH):
            nxt = prod_units(c, ch + 1) if ch + 1 < NCH else []
            emit_mamba_scan(c, ch, nxt)

        if debug == "mamba":
            c.wait("sync", c.dL, c.dL.n)
            c.wait("sync", c.dSc, 16 * NCH)
            c.wait("gpsimd", c.CC, NCH)
        else:
            A.reset()
            A.mark()
            c.ones_tf = A.f32(128, TF)
            c.xf_sb = A.bf16(128, DMT * TF)
            c.c_sb = A.f32(128, DMT * TF)
            c.cbf = A.bf16(128, DMT * TF)
            c.gate_w = {}
            for g in ("f", "i", "g"):
                c.gate_w[g] = (A.bf16(128, DMT * 1024), A.bf16(128, DMT * 1024))
            c.wo_sb = A.bf16(128, DMT * 1024)
            c.uo_sb = A.bf16(128, DMT * 1024)
            c.wv_sb = c.wo_sb
            c.woa_sb = c.uo_sb
            c.gbuf = {g: A.f32(128, TF) for g in ("f", "i", "g")}
            c.hbuf = [A.f32(128, TF), c.c_sb[:, 5 * TF:6 * TF]]
            c.obuf = [A.f32(128, TF) for _ in range(2)]
            c.res_buf = [A.f32(128, TF) for _ in range(2)]
            c.tmpf = A.f32(128, TF)
            c.tmpf2 = A.f32(128, TF)
            c.xtile = [A.f32(128, TF) for _ in range(2)]
            _sq = c.tmpf.bitcast(BF16)
            c.xsqt = [_sq[:, 0:TF], _sq[:, TF:2 * TF]]
            c.xnt = c.tmpf2
            c.cuo_bf = c.c_sb.bitcast(BF16)[:, 0:DMT * TF]
            c.vsumP = A.f32(128, 64)
            c.vsum_bf = A.bf16(128, 64)
            c.av_bf = A.bf16(128, 64)
            c.attnv = A.f32(128, 64)
            c.xtile_free_t = [0, 0]
            c.xtile_free_v = [0, 0]
            c.xtile_store_w = [None, None]
            c.xf_free_v = 0
            c.c0_free_v = 0
            c.res_free_v = [0, 0]
            c.hbuf_free_v = [0, 0]
            c.obuf_w = [None, None]
            c.xf_store_w = [None] * B
            c.x2_store_w = [[None] * DMT for _ in range(B)]
            c.x3_store_w = [[None] * DMT for _ in range(B)]
            c.wouo_w = []
            c.cbf_free_t = 0
            emit_phase2(c, out)
            c.wait("sync", c.dL, c.dL.n)
            c.pipe_drain("sync")

        block = e(nc.Block())
        for eng_name in ENGINES:
            getattr(block, eng_name)(
                lambda E, ops=tuple(c.ops[eng_name]): [f(E) for f in ops])
    return nc



def _pad_wxp(wxp):
    out = np.zeros((DI, 128), np.float32)
    out[:, 0:16] = wxp[:, 64:80]
    out[:, 32:48] = wxp[:, 80:96]
    out[:, 64:128] = wxp[:, 0:64]
    return out


def _stage(inputs):
    f32 = np.float32
    x = np.asarray(inputs["L0_attn_a_x"], f32)
    convw = np.asarray(inputs["L0_attn_a_conv_w"], f32)
    wqkv = np.asarray(inputs["L0_attn_b_w_qkv"], f32)
    alog = np.asarray(inputs["L0_attn_a_A_log"], f32)

    convd = np.zeros((128, 64 * 128), f32)
    for kt in range(DIT):
        for tap in range(4):
            blk = (kt * 4 + tap) * 128
            convd[np.arange(128), blk + np.arange(128)] = convw[kt * 128:(kt + 1) * 128, tap]

    sel16 = np.zeros((16, 16 * 128), f32)
    for n in range(16):
        sel16[n, n * 128:(n + 1) * 128] = 1.0

    def cols(v, nt):
        return np.ascontiguousarray(np.asarray(v, f32).reshape(nt, 128).T)

    def ct(v):
        return np.ascontiguousarray(np.asarray(v, f32).T)

    common = {
        "w_in": np.ascontiguousarray(np.asarray(inputs["L0_attn_a_W_in"], f32)),
        "w_xp": _pad_wxp(np.asarray(inputs["L0_attn_a_W_xproj"], f32)),
        "w_dt": np.ascontiguousarray(np.asarray(inputs["L0_attn_a_W_dt"], f32)),
        "w_out": np.ascontiguousarray(np.asarray(inputs["L0_attn_a_W_out"], f32)),
        "convd": convd,
        "alog": np.ascontiguousarray(
            alog.reshape(DIT, 128, DS).transpose(1, 0, 2).reshape(128, 256)),
        "gamma8": cols(inputs["L0_attn_a_ln_gamma"], DMT),
        "beta8": cols(inputs["L0_attn_a_ln_beta"], DMT),
        "convb16": cols(inputs["L0_attn_a_conv_b"], DIT),
        "bdt16": cols(inputs["L0_attn_a_b_dt"], DIT),
        "dsk16": cols(inputs["L0_attn_a_D_skip"], DIT),
        "bf8": cols(inputs["L0_ffn_bf"], DMT), "bi8": cols(inputs["L0_ffn_bi"], DMT),
        "bc8": cols(inputs["L0_ffn_bc"], DMT), "bo8": cols(inputs["L0_ffn_bo"], DMT),
        "sel16": sel16, "ident": np.eye(128, dtype=f32),
        "c0": np.ascontiguousarray(np.broadcast_to(
            np.asarray(inputs["L0_ffn_in_c"], f32)[:, None], (DM, TF))),
        "wfT": ct(inputs["L0_ffn_wf"]), "ufT": ct(inputs["L0_ffn_uf"]),
        "wiT": ct(inputs["L0_ffn_wi"]), "uiT": ct(inputs["L0_ffn_ui"]),
        "wcT": ct(inputs["L0_ffn_wc"]), "ucT": ct(inputs["L0_ffn_uc"]),
        "woT": ct(inputs["L0_ffn_wo"]), "uoT": ct(inputs["L0_ffn_uo"]),
        "wvT": np.ascontiguousarray(wqkv[2 * DM:3 * DM, :].T),
        "woaT": ct(inputs["L0_attn_b_w_o"]),
    }
    in_maps = []
    for i in range(NCORES):
        m = dict(common)
        m["xT"] = np.ascontiguousarray(x[i].T)
        gidx = np.zeros((128, 128), np.int32)
        for b in range(B):
            for kt in range(DMT):
                for h2 in range(2):
                    cc = 2 * i + h2
                    gidx[:, (b * DMT + kt) * 2 + h2] = (
                        (cc * NCORES + b) * DM + kt * 128 + np.arange(128))
        m["gidx"] = gidx
        in_maps.append(m)
    return in_maps


def kernel(**inputs):
    in_maps = _stage(inputs)
    nc = build()
    res = run_bass_kernel_spmd(nc, in_maps, core_ids=list(range(NCORES)))
    full = np.empty((B, L, DM), np.float32)
    for i in range(NCORES):
        o = res.results[i]["out"]
        full[:, TF * i:TF * (i + 1), :] = o.transpose(0, 2, 1)
    return full
